# version 10
# baseline (speedup 1.0000x reference)
import os
os.environ.setdefault("TRN_ALLOW_ICI_ALL_TO_ALL", "1")

import numpy as np
import ml_dtypes
import concourse.bass as bass
import concourse.mybir as mybir
from concourse.bass_utils import run_bass_kernel_spmd

F32 = mybir.dt.float32
BF16 = mybir.dt.bfloat16
AF = mybir.ActivationFunctionType
OP = mybir.AluOpType
AX = mybir.AxisListType
BF = ml_dtypes.bfloat16

D = 4096
KC = 32
TOKQ = 1024
HALO = 4
W = TOKQ + HALO
NH, NKV, HD = 16, 4, 64
LAT, KVD = 1024, 256
S = 4096
EPS = 1e-5
NQG, NKT = 8, 32

SH1_Q, SH1_K, SH1_V = 0, 1024, 2048
SH1C = 2568

_PHASES = int(os.environ.get("MK_PHASES", "3"))
_FF = int(os.environ.get("MK_FF", "5"))
_LN = int(os.environ.get("MK_LN", "4"))


def build():
    nc = bass.Bass("TRN2", target_bir_lowering=False)

    d_xT = nc.dram_tensor("xT", [D, W], F32, kind="ExternalInput").ap()
    d_base = nc.dram_tensor("base_b", [KC, 128, KC * 128], F32, kind="ExternalInput").ap()
    d_bexp = nc.dram_tensor("bexp_b", [KC, 128, 8 * 128], F32, kind="ExternalInput").ap()
    d_ar = nc.dram_tensor("ar_b", [128, KC * 128], F32, kind="ExternalInput").ap()
    d_wqkv = nc.dram_tensor("wqkv_b", [12, 128, KC * 128], F32, kind="ExternalInput").ap()
    d_wo = nc.dram_tensor("wo_b", [KC, 128, 8 * 128], F32, kind="ExternalInput").ap()
    d_qc2 = nc.dram_tensor("qc2_b", [8, 128, 8 * 3 * 128], F32, kind="ExternalInput").ap()
    d_qc1 = nc.dram_tensor("qc1_b", [128, 8 * 3 * 64], F32, kind="ExternalInput").ap()
    d_kc1 = nc.dram_tensor("kc1_b", [128, 2 * 3 * 64], F32, kind="ExternalInput").ap()
    d_kc2 = nc.dram_tensor("kc2_b", [128, 2 * 2 * 3 * 128], F32, kind="ExternalInput").ap()
    d_qc1bias = nc.dram_tensor("qc1_bias", [128, 8], F32, kind="ExternalInput").ap()
    d_kc1bias = nc.dram_tensor("kc1_bias", [128, 2], F32, kind="ExternalInput").ap()
    d_qc2bias = nc.dram_tensor("qc2_bias", [128, 8], F32, kind="ExternalInput").ap()
    d_kc2bias = nc.dram_tensor("kc2_bias", [128, 2], F32, kind="ExternalInput").ap()
    d_bbias = nc.dram_tensor("base_bias", [128, 32], F32, kind="ExternalInput").ap()
    d_ktemp = nc.dram_tensor("key_temp", [1], F32, kind="ExternalInput").ap()
    d_ident = nc.dram_tensor("ident_bf", [128, 128], BF16, kind="ExternalInput").ap()
    d_perm = nc.dram_tensor("perm_bf", [128, 128], BF16, kind="ExternalInput").ap()
    d_sumhead = nc.dram_tensor("sumhead_bf", [128, 64], BF16, kind="ExternalInput").ap()
    d_ones1 = nc.dram_tensor("ones1_bf", [128, 1], BF16, kind="ExternalInput").ap()
    d_ones2 = nc.dram_tensor("ones2_bf", [128, 2], BF16, kind="ExternalInput").ap()
    d_pairsel = nc.dram_tensor("pairsel_bf", [16, 8 * 128], BF16, kind="ExternalInput").ap()
    d_oner64 = nc.dram_tensor("oner64_bf", [128, 64], BF16, kind="ExternalInput").ap()
    d_masks = nc.dram_tensor("masks_bf", [128, 4 * 512], BF16, kind="ExternalInput").ap()
    d_cos = nc.dram_tensor("rope_cos", [128, TOKQ], F32, kind="ExternalInput").ap()
    d_sin = nc.dram_tensor("rope_sin", [128, TOKQ], F32, kind="ExternalInput").ap()
    d_hscale = nc.dram_tensor("halo_scale", [1], F32, kind="ExternalInput").ap()

    d_yT = nc.dram_tensor("yT", [D, TOKQ], F32, kind="ExternalOutput").ap()

    d_bc = nc.dram_tensor("bc_scratch", [8, 1040], F32).ap()
    d_cb = {
        "base": nc.dram_tensor("c_base", [KC, 128, KC * 128], BF16).ap(),
        "bexp": nc.dram_tensor("c_bexp", [KC, 128, 8 * 128], BF16).ap(),
        "ar": nc.dram_tensor("c_ar", [128, KC * 128], BF16).ap(),
        "qkv": nc.dram_tensor("c_wqkv", [12, 128, KC * 128], BF16).ap(),
        "qc2": nc.dram_tensor("c_qc2", [8, 128, 8 * 3 * 128], BF16).ap(),
        "wo": nc.dram_tensor("c_wo", [KC, 128, 8 * 128], BF16).ap(),
    }
    d_a1inA = nc.dram_tensor("a2a1_inA", [8, 128, 1284], BF16).ap()
    d_a1outA = nc.dram_tensor("a2a1_outA", [8, 128, 1284], BF16).ap()
    d_a1inB = nc.dram_tensor("a2a1_inB", [8, 128, 1284], BF16).ap()
    d_a1outB = nc.dram_tensor("a2a1_outB", [8, 128, 1284], BF16).ap()
    d_a2in = nc.dram_tensor("a2a2_in", [8, 128, TOKQ], BF16).ap()
    d_a2out = nc.dram_tensor("a2a2_out", [8, 128, TOKQ], BF16).ap()

    sb_xT = nc.alloc_sbuf_tensor("sb_xT", [128, KC * W], F32).ap()
    sb_stage = nc.alloc_sbuf_tensor("sb_stage", [128, 4 * 512], F32).ap()
    sb_wbf = nc.alloc_sbuf_tensor("sb_wbf", [128, 4 * 512], BF16).ap()
    sb_ident = nc.alloc_sbuf_tensor("sb_ident", [128, 128], BF16).ap()
    sb_perm = nc.alloc_sbuf_tensor("sb_perm", [128, 128], BF16).ap()
    sb_sumhead = nc.alloc_sbuf_tensor("sb_sumhead", [128, 64], BF16).ap()
    sb_ones1 = nc.alloc_sbuf_tensor("sb_ones1", [128, 1], BF16).ap()
    sb_ones2 = nc.alloc_sbuf_tensor("sb_ones2", [128, 2], BF16).ap()
    sb_pairsel = nc.alloc_sbuf_tensor("sb_pairsel", [16, 8 * 128], BF16).ap()
    sb_oner64 = nc.alloc_sbuf_tensor("sb_oner64", [128, 64], BF16).ap()
    sb_qc1bias = nc.alloc_sbuf_tensor("sb_qc1bias", [128, 8], F32).ap()
    sb_kc1bias = nc.alloc_sbuf_tensor("sb_kc1bias", [128, 2], F32).ap()
    sb_qc2bias = nc.alloc_sbuf_tensor("sb_qc2bias", [128, 8], F32).ap()
    sb_kc2bias = nc.alloc_sbuf_tensor("sb_kc2bias", [128, 2], F32).ap()
    sb_bbh = nc.alloc_sbuf_tensor("sb_bbh", [128, 32], F32).ap()
    sb_ktcol = nc.alloc_sbuf_tensor("sb_ktcol", [128, 1], F32).ap()
    sb_expbias = nc.alloc_sbuf_tensor("sb_expbias", [128, 1], F32).ap()
    sb_hscale = nc.alloc_sbuf_tensor("sb_hscale", [128, 1], F32).ap()
    sb_eps = nc.alloc_sbuf_tensor("sb_eps", [128, 1], F32).ap()

    ARN = 29456
    sb_arena = nc.alloc_sbuf_tensor("sb_arena", [128, ARN], BF16).ap()

    Pp = [nc.alloc_psum_tensor(f"p{i}", [128, 512], F32).ap() for i in range(7)]
    PT = nc.alloc_psum_tensor("pt", [128, 1024], BF16).ap()

    def abf(off, cols):
        return sb_arena[:, off:off + cols]

    def af32(off, cols):
        return sb_arena[:, off:off + 2 * cols].bitcast(F32)

    a_ln = abf(0, KC * 516)
    o = 0
    a_means = abf(o, 1024); o += 1024
    a_t1 = af32(o, 512); o += 1024
    a_t2 = af32(o, 512); o += 1024
    a_sq = abf(o, 512); o += 512
    a_rr = af32(o, 512); o += 1024
    a_rstdb = af32(o, 512); o += 1024
    a_cv = abf(o, 2 * 512); o += 1024
    a_qf = abf(o, 512); o += 512
    a_vt = abf(o, 2 * 132); o += 264
    a_ropec = af32(o, 512); o += 1024
    a_ropes = af32(o, 512); o += 1024
    OV_SH = o
    o = OV_SH
    a_k1 = abf(o, 2 * 514); o += 1028
    a_kc1w = abf(o, 2 * 3 * 64); o += 384
    a_kc2w = abf(o, 2 * 2 * 3 * 128); o += 1536
    assert o <= 16512, o
    o = OV_SH
    a_q1 = abf(o, 8 * 514); o += 8 * 514
    a_qc1w = abf(o, 8 * 3 * 64); o += 1536
    assert o <= 16512, o

    o = 16512
    a_xbf = abf(o, 2 * 520); o += 1040
    a_x2 = abf(o, 520); o += 520
    a_abc = af32(o, 516); o += 1032
    a_bbc = af32(o, 516); o += 1032
    a_rows = af32(o, 1032); o += 2064
    a_lnt = af32(o, 516); o += 1032
    a_cs = af32(o, 8); o += 16
    assert o <= 23256, o

    C0 = 23256
    a_qkv = abf(C0, 12 * 516)
    o = C0
    a_h2 = abf(o, 516); o += 528
    a_lg = abf(o, 516); o += 528
    a_lt = abf(o, 96); o += 96
    a_le = abf(o, 96); o += 96
    a_cmb = abf(o, 516); o += 528
    CH0 = o
    a_ch = [abf(CH0 + i * 516, 516) for i in range(8)]
    o += 8 * 516
    assert o <= C0 + 6192, o
    a_af = abf(C0, 8 * 512)

    o = 0
    t_K = abf(o, S); o += S
    t_Q = abf(o, S); o += S
    t_VT = abf(o, NKT * 66); o += NKT * 66
    t_masks = abf(o, 4 * 512); o += 2048
    t_at = abf(o, 4 * 512); o += 2048
    t_ao = abf(o, 2 * 512); o += 1024
    t_dn = af32(o, 1024); o += 2048
    t_rdb = af32(o, 1024); o += 2048
    t_dnh = abf(o, 1024); o += 1024
    t_dnl = abf(o, 1024); o += 1024
    assert o <= ARN, o

    EV = {"sync": [], "tensor": [], "vector": [], "scalar": [], "gpsimd": []}
    ENGS = list(EV)

    def emit(eng, f):
        EV[eng].append(f)

    class Ctr:
        def __init__(s, name):
            s.h = nc.alloc_semaphore(name=name)
            s.v = 0

        def inc(s, n=1):
            s.v += n
            return s.v

    sW, sCV = Ctr("sW"), Ctr("sCV")
    sX, sPE, sDV, sAC = Ctr("sX"), Ctr("sPE"), Ctr("sDV"), Ctr("sAC")
    sCC, sST = Ctr("sCC"), Ctr("sST")
    sR = [Ctr("sR0"), Ctr("sR1"), Ctr("sR2"), Ctr("sR3")]

    ECTR = {"tensor": sPE, "vector": sDV, "scalar": sAC, "gpsimd": sX, "sync": sW}
    ALLC = [sW, sCV, sX, sPE, sDV, sAC, sCC, sST, sR[0], sR[1], sR[2], sR[3]]

    def wait(eng, ctr, tgt=None):
        if tgt is None:
            ctr, tgt = ctr
        if tgt <= 0:
            return
        emit(eng, lambda e, h=ctr.h, t=tgt: e.wait_ge(h, t))

    def barrier():
        for e in ENGS:
            for c in ALLC:
                wait(e, c, c.v)

    def dma(eng, out_ap, in_ap, ctr=None):
        c = ctr if ctr is not None else (sW if eng == "sync" else sX)
        c.inc(16)
        emit(eng, lambda e, oa=out_ap, ia=in_ap, h=c.h: e.dma_start(out=oa, in_=ia).then_inc(h, 16))
        return c.v

    def op(eng, ctr, f):
        ctr.inc(1)
        emit(eng, lambda e, f=f, h=ctr.h: f(e).then_inc(h, 1))
        return ctr.v

    def opq(eng, f):
        return op(eng, ECTR[eng], f)

    def MM(out, lhsT, rhs, start, stop, tp=None, inc=False):
        if inc:
            sPE.inc(1)
            emit("tensor", lambda e, oo=out, l=lhsT, r=rhs, a=start, b=stop, t=tp, h=sPE.h:
                 e.matmul(oo, l, r, start=a, stop=b, tile_position=t).then_inc(h, 1))
        else:
            emit("tensor", lambda e, oo=out, l=lhsT, r=rhs, a=start, b=stop, t=tp:
                 e.matmul(oo, l, r, start=a, stop=b, tile_position=t))
        return sPE.v

    def now(eng):
        return ECTR[eng].v

    def selfsync(eng):
        c = ECTR[eng]
        wait(eng, c, c.v)

    bc_i = [0]

    def bcast(row_ap, n, outs):
        sl = bc_i[0] % 8
        bc_i[0] += 1
        t1 = dma("gpsimd", d_bc[sl:sl + 1, 0:n], row_ap)
        wait("gpsimd", sX, t1)
        for out_ap, cnt in outs:
            dma("gpsimd", out_ap, d_bc[sl, 0:n].partition_broadcast(cnt))
        wait("gpsimd", sX, sX.v)
        return sX.v

    ring_i = [0]
    ring_fin = {}
    ring_fp = [0]
    _cached = set()
    _stage_last = {0: None, 1: None, 2: None, 3: None}

    def ring_load(piece_ap, ncols, scale=1.0, eng="vector"):
        i = ring_i[0]
        ring_i[0] += 1
        st, wb = i % 4, i % 4
        if _stage_last[st] is not None:
            wait("sync", *_stage_last[st])
        tgt = dma("sync", sb_stage[:, st * 512: st * 512 + ncols], piece_ap, ctr=sR[wb])
        wait(eng, sR[wb], tgt)
        wait(eng, sCV, sCV.v)
        if i >= 4:
            for ctr, val in ring_fin[i - 4]:
                wait(eng, ctr, val)
        if eng == "vector":
            cv = op("vector", sCV, lambda e, st=st, wb=wb, nn=ncols, sc=scale: e.tensor_scalar(
                sb_wbf[:, wb * 512: wb * 512 + nn], sb_stage[:, st * 512: st * 512 + nn],
                sc, None, OP.mult))
        else:
            cv = op("scalar", sCV, lambda e, st=st, wb=wb, nn=ncols, sc=scale: e.activation(
                sb_wbf[:, wb * 512: wb * 512 + nn], sb_stage[:, st * 512: st * 512 + nn],
                AF.Copy, scale=sc))
        _stage_last[st] = (sCV, cv)
        return sb_wbf[:, wb * 512:(wb + 1) * 512], cv

    ring_extra = {}

    def ring_done(eng="tensor"):
        fins = [(ECTR[eng], ECTR[eng].v)]
        if ring_fp[0] in ring_extra:
            fins.append(ring_extra.pop(ring_fp[0]))
        ring_fin[ring_fp[0]] = fins
        ring_fp[0] += 1

    def ring_load_c(key, pidx, f32_piece, bf_piece, ncols, scale=1.0, eng="vector"):
        i = ring_i[0]
        ring_i[0] += 1
        st, wb = i % 4, i % 4
        wslot = sb_wbf[:, wb * 512: wb * 512 + ncols]
        if (key, pidx) not in _cached:
            _cached.add((key, pidx))
            if _stage_last[st] is not None:
                wait("sync", *_stage_last[st])
            tgt = dma("sync", sb_stage[:, st * 512: st * 512 + ncols], f32_piece, ctr=sR[wb])
            wait(eng, sR[wb], tgt)
            wait(eng, sCV, sCV.v)
            if i >= 4:
                for ctr, val in ring_fin[i - 4]:
                    wait(eng, ctr, val)
            if eng == "vector":
                cv = op("vector", sCV, lambda e, w=wslot, st=st, nn=ncols, sc=scale: e.tensor_scalar(
                    w, sb_stage[:, st * 512: st * 512 + nn], sc, None, OP.mult))
            else:
                cv = op("scalar", sCV, lambda e, w=wslot, st=st, nn=ncols, sc=scale: e.activation(
                    w, sb_stage[:, st * 512: st * 512 + nn], AF.Copy, scale=sc))
            _stage_last[st] = (sCV, cv)
            wait("gpsimd", sCV, cv)
            wbv = dma("gpsimd", bf_piece, wslot)
            wait("gpsimd", sX, wbv)
            ring_extra[i] = (sX, wbv)
            return wslot, (sCV, cv)
        if i >= 4:
            for ctr, val in ring_fin[i - 4]:
                wait("sync", ctr, val)
        tgt = dma("sync", wslot, bf_piece, ctr=sR[wb])
        return wslot, (sR[wb], tgt)

    def load_const_bf(dst_bf, dram_ap, total):
        offc = 0
        while offc < total:
            n = min(512, total - offc)
            piece, cv = ring_load(dram_ap[:, offc:offc + n], n)
            wait("vector", sCV, cv)
            opq("vector", lambda e, p=piece, dd=dst_bf, oo=offc, n=n: e.tensor_copy(
                dd[:, oo:oo + n], p[:, 0:n]))
            ring_done("vector")
            offc += n
        return now("vector")

    for k in range(KC):
        dma("gpsimd", sb_xT[:, k * W:(k + 1) * W], d_xT[k * 128:(k + 1) * 128, :])
    dma("sync", sb_ident[:, :], d_ident[:, :])
    dma("sync", sb_perm[:, :], d_perm[:, :])
    dma("sync", sb_sumhead[:, :], d_sumhead[:, :])
    dma("sync", sb_ones1[:, :], d_ones1[:, :])
    dma("sync", sb_ones2[:, :], d_ones2[:, :])
    dma("sync", sb_pairsel[:, :], d_pairsel[:, :])
    dma("sync", sb_oner64[:, :], d_oner64[:, :])
    dma("sync", sb_qc1bias[:, :], d_qc1bias[:, :])
    dma("sync", sb_kc1bias[:, :], d_kc1bias[:, :])
    dma("sync", sb_qc2bias[:, :], d_qc2bias[:, :])
    dma("sync", sb_kc2bias[:, :], d_kc2bias[:, :])
    dma("sync", sb_bbh[:, :], d_bbias[:, :])
    dma("sync", sb_ktcol[:, :], d_ktemp[0:1].partition_broadcast(128))
    t_ld = dma("sync", sb_hscale[:, :], d_hscale[0:1].partition_broadcast(128))
    wait("vector", sW, t_ld)
    opq("vector", lambda e: e.tensor_scalar(sb_bbh[:, :], sb_bbh[:, :], 0.5, None, OP.mult))
    opq("vector", lambda e: e.memset(sb_eps[:, :], EPS))
    wait("scalar", sW, t_ld)
    opq("scalar", lambda e: e.activation(sb_expbias[:, :], sb_ktcol[:, :], AF.Abs))
    selfsync("scalar")
    opq("scalar", lambda e: e.mul(sb_expbias[:, :], sb_expbias[:, :], -0.125))
    barrier()

    def layer_norm(base, cw, blocks):
        pe_slot = {-2: 0, -1: 0}
        for k in range(KC):
            sl = k % 2
            xb = a_xbf[:, sl * 520: sl * 520 + cw]
            x2 = a_x2[:, 0:cw]
            wait("vector", sPE, pe_slot[k - 2])
            cvt = opq("vector", lambda e, k=k, xb=xb: e.tensor_copy(
                xb, sb_xT[:, k * W + base: k * W + base + cw]))
            wait("scalar", sDV, cvt)
            wait("scalar", sPE, pe_slot[k - 1])
            sqt = opq("scalar", lambda e, xb=xb, x2=x2: e.activation(x2, xb, AF.Square))
            wait("tensor", sDV, cvt)
            wait("tensor", sAC, sqt)
            for bi, (b0, bw) in enumerate(blocks):
                off = b0 - base
                MM(Pp[2 * bi][0:1, 0:bw], sb_ones1[:, :], xb[:, off:off + bw], k == 0, k == KC - 1)
                MM(Pp[2 * bi + 1][0:1, 0:bw], sb_ones1[:, :], x2[:, off:off + bw],
                   k == 0, k == KC - 1, inc=(bi == len(blocks) - 1))
            pe_slot[k] = sPE.v
        pe_done = pe_slot[KC - 1]
        mrow = a_rows[0:1, 0:516]
        vrow = a_rows[0:1, 516:1032]
        trow = a_lnt[0:1, 0:516]
        lo = blocks[0][0] - base
        hi = blocks[-1][0] + blocks[-1][1] - base
        nb = hi - lo
        wait("vector", sPE, pe_done)
        wait("vector", sX, sX.v)
        for bi, (b0, bw) in enumerate(blocks):
            off = b0 - base
            opq("vector", lambda e, bi=bi, off=off, bw=bw: e.tensor_scalar(
                mrow[:, off:off + bw], Pp[2 * bi][0:1, 0:bw], 1.0 / D, None, OP.mult))
            opq("vector", lambda e, bi=bi, off=off, bw=bw: e.tensor_scalar(
                vrow[:, off:off + bw], Pp[2 * bi + 1][0:1, 0:bw], 1.0 / D, None, OP.mult))
        psum_free = now("vector")
        selfsync("vector")
        opq("vector", lambda e, lo=lo, nb=nb: e.tensor_tensor(
            trow[:, lo:lo + nb], mrow[:, lo:lo + nb], mrow[:, lo:lo + nb], OP.mult))
        selfsync("vector")
        dvv = opq("vector", lambda e, lo=lo, nb=nb: e.tensor_tensor(
            vrow[:, lo:lo + nb], vrow[:, lo:lo + nb], trow[:, lo:lo + nb], OP.subtract))
        wait("scalar", sDV, dvv)
        opq("scalar", lambda e, lo=lo, nb=nb: e.activation(
            vrow[:, lo:lo + nb], vrow[:, lo:lo + nb], AF.Sqrt, bias=sb_eps[0:1, 0:1]))
        selfsync("scalar")
        opq("scalar", lambda e, lo=lo, nb=nb: e.activation(
            vrow[:, lo:lo + nb], vrow[:, lo:lo + nb], AF.Ln))
        selfsync("scalar")
        sct = opq("scalar", lambda e, lo=lo, nb=nb: e.activation(
            vrow[:, lo:lo + nb], vrow[:, lo:lo + nb], AF.Exp, scale=-1.0))
        wait("vector", sAC, sct)
        opq("vector", lambda e, lo=lo, nb=nb: e.tensor_tensor(
            mrow[:, lo:lo + nb], mrow[:, lo:lo + nb], vrow[:, lo:lo + nb], OP.mult))
        selfsync("vector")
        dvr = opq("vector", lambda e, lo=lo, nb=nb: e.tensor_scalar(
            mrow[:, lo:lo + nb], mrow[:, lo:lo + nb], -1.0, None, OP.mult))
        wait("gpsimd", sDV, dvr)
        sl = bc_i[0] % 8
        bc_i[0] += 1
        t1 = dma("gpsimd", d_bc[sl:sl + 1, 0:1032], a_rows[0:1, 0:1032])
        wait("gpsimd", sX, t1)
        dma("gpsimd", a_abc[:, lo:lo + nb], d_bc[sl, 516 + lo:516 + hi].partition_broadcast(128))
        dma("gpsimd", a_bbc[:, lo:lo + nb], d_bc[sl, lo:hi].partition_broadcast(128))
        wait("gpsimd", sX, sX.v)
        bc_done = sX.v
        wait("vector", sX, bc_done)
        apply_dv = []
        for k in range(KC):
            for (b0, bw) in blocks:
                off = b0 - base
                selfsync("vector")
                opq("vector", lambda e, k=k, b0=b0, off=off, bw=bw: e.tensor_tensor(
                    a_lnt[:, off:off + bw], sb_xT[:, k * W + b0: k * W + b0 + bw],
                    a_abc[:, off:off + bw], OP.mult))
                selfsync("vector")
                opq("vector", lambda e, k=k, off=off, bw=bw: e.tensor_tensor(
                    a_ln[:, k * 516 + off: k * 516 + off + bw], a_lnt[:, off:off + bw],
                    a_bbc[:, off:off + bw], OP.add))
            apply_dv.append(now("vector"))
        return apply_dv, psum_free

    def ffn(call):
        for g in range(2):
            if g == 0:
                base, cw = 0, 516
                blocks = ([(0, 4), (4, 512)] if call == 0 else [(4, 512)])
            else:
                base, cw = 516, 512
                blocks = [(516, 512)]
            mainbi = len(blocks) - 1
            apply_dv, psum_free = layer_norm(base, cw, blocks)
            if _FF < 2:
                continue

            for pc in range(8):
                piece, cv = ring_load_c("ar", pc, d_ar[:, pc * 512:(pc + 1) * 512],
                                        d_cb["ar"][:, pc * 512:(pc + 1) * 512], 512, eng="scalar")
                wait("tensor", cv)
                wait("tensor", sDV, apply_dv[pc * 4 + 3])
                for kk in range(4):
                    k = pc * 4 + kk
                    for bi, (b0, bw) in enumerate(blocks):
                        off = b0 - base
                        MM(Pp[4 + (bi != mainbi)][0:80, 0:bw], piece[:, kk * 128: kk * 128 + 80],
                           a_ln[:, k * 516 + off: k * 516 + off + bw], k == 0, k == KC - 1,
                           inc=(kk == 3 and bi == len(blocks) - 1))
                ring_done()
            pe_ar = now("tensor")

            wait("scalar", sPE, pe_ar)
            for bi, (b0, bw) in enumerate(blocks):
                off = b0 - base
                pb = Pp[4 + (bi != mainbi)]
                opq("scalar", lambda e, pb=pb, off=off, bw=bw: e.activation(
                    a_h2[0:64, off:off + bw], pb[0:64, 0:bw], AF.Copy))
                opq("scalar", lambda e, pb=pb, off=off, bw=bw: e.activation(
                    a_lg[64:80, off:off + bw], pb[64:80, 0:bw], AF.Copy))
            ac_ev = now("scalar")
            wait("gpsimd", sAC, ac_ev)
            o0 = blocks[0][0] - base
            ow = (blocks[-1][0] + blocks[-1][1]) - blocks[0][0]
            h2d = dma("gpsimd", a_h2[64:128, o0:o0 + ow], a_h2[0:64, o0:o0 + ow])
            wait("gpsimd", sX, h2d)

            if _FF < 3:
                continue
            nblk = (ow + 127) // 128
            wait("tensor", sAC, ac_ev)
            for t in range(nblk):
                tw = min(128, ow - t * 128)
                opq("tensor", lambda e, t=t, tw=tw, o0=o0: e.transpose(
                    PT[0:tw, t * 16: t * 16 + 16],
                    a_lg[64:80, o0 + t * 128: o0 + t * 128 + tw], sb_ident[64:80, 64:80]))
            pe_tr = now("tensor")
            wait("vector", sPE, pe_tr)
            for t in range(nblk):
                tw = min(128, ow - t * 128)
                opq("vector", lambda e, t=t, tw=tw: e.tensor_copy(
                    a_lt[0:tw, t * 16: t * 16 + 16], PT[0:tw, t * 16: t * 16 + 16]))
            dv_lt = now("vector")
            for t in range(nblk):
                tw = min(128, ow - t * 128)
                lt = a_lt[0:tw, t * 16: t * 16 + 16]
                le = a_le[0:tw, t * 16: t * 16 + 16]
                m0 = a_cs[0:tw, 0:1]; zz = a_cs[0:tw, 1:2]; m1 = a_cs[0:tw, 2:3]
                m2 = a_cs[0:tw, 3:4]; dd = a_cs[0:tw, 4:5]; w1 = a_cs[0:tw, 5:6]; w2 = a_cs[0:tw, 6:7]
                selfsync("vector")
                opq("vector", lambda e, lt=lt, m0=m0: e.tensor_reduce(m0, lt, AX.X, OP.max))
                selfsync("vector")
                dv1 = opq("vector", lambda e, lt=lt, m0=m0: e.tensor_scalar(
                    lt, lt, m0, None, OP.subtract))
                wait("scalar", sDV, dv1)
                ac1 = opq("scalar", lambda e, lt=lt, le=le, zz=zz: e.activation(
                    le, lt, AF.Exp, accum_out=zz))
                wait("vector", sAC, ac1)
                opq("vector", lambda e, le=le, m1=m1: e.tensor_reduce(m1, le, AX.X, OP.max))
                selfsync("vector")
                opq("vector", lambda e, le=le, m1=m1, lt=lt: e.tensor_scalar(
                    lt, le, m1, None, OP.is_equal))
                selfsync("vector")
                opq("vector", lambda e, le=le, lt=lt: e.tensor_tensor(lt, lt, le, OP.mult))
                selfsync("vector")
                opq("vector", lambda e, le=le, lt=lt: e.tensor_tensor(lt, le, lt, OP.subtract))
                selfsync("vector")
                opq("vector", lambda e, lt=lt, m2=m2: e.tensor_reduce(m2, lt, AX.X, OP.max))
                opq("vector", lambda e, zz=zz, dd=dd: e.tensor_scalar(dd, zz, 1e-6, None, OP.mult))
                selfsync("vector")
                opq("vector", lambda e, dd=dd, m1=m1: e.tensor_tensor(dd, dd, m1, OP.add))
                selfsync("vector")
                opq("vector", lambda e, dd=dd, m2=m2: e.tensor_tensor(dd, dd, m2, OP.add))
                selfsync("vector")
                opq("vector", lambda e, dd=dd: e.reciprocal(dd, dd))
                selfsync("vector")
                opq("vector", lambda e, dd=dd: e.tensor_scalar(dd, dd, 0.25, None, OP.mult))
                selfsync("vector")
                opq("vector", lambda e, m1=m1, dd=dd, w1=w1: e.tensor_tensor(w1, m1, dd, OP.mult))
                opq("vector", lambda e, m2=m2, dd=dd, w2=w2: e.tensor_tensor(w2, m2, dd, OP.mult))
                selfsync("vector")
                opq("vector", lambda e, le=le, m1=m1, lt=lt: e.tensor_scalar(
                    lt, le, m1, None, OP.is_equal))
                selfsync("vector")
                opq("vector", lambda e, lt=lt, w1=w1: e.tensor_scalar(lt, lt, w1, None, OP.mult))
                opq("vector", lambda e, le=le, m2=m2: e.tensor_scalar(le, le, m2, None, OP.is_equal))
                selfsync("vector")
                opq("vector", lambda e, le=le, w2=w2: e.tensor_scalar(le, le, w2, None, OP.mult))
                selfsync("vector")
                opq("vector", lambda e, lt=lt, le=le: e.tensor_tensor(le, lt, le, OP.add))
            dv_tk = now("vector")
            wait("tensor", sDV, dv_tk)
            for t in range(nblk):
                tw = min(128, ow - t * 128)
                opq("tensor", lambda e, t=t, tw=tw: e.transpose(
                    PT[0:16, t * 128: t * 128 + tw],
                    a_le[0:tw, t * 16: t * 16 + 16], sb_ident[0:tw, 0:tw]))
            pe_tc = now("tensor")
            wait("scalar", sPE, pe_tc)
            for t in range(nblk):
                tw = min(128, ow - t * 128)
                opq("scalar", lambda e, t=t, tw=tw, o0=o0: e.activation(
                    a_cmb[0:16, o0 + t * 128: o0 + t * 128 + tw],
                    PT[0:16, t * 128: t * 128 + tw], AF.Copy))
            ac_cmb = now("scalar")

            if _FF < 4:
                continue
            wait("tensor", sAC, ac_cmb)
            wait("tensor", sX, h2d)
            wait("tensor", sDV, psum_free)
            wait("vector", sAC, ac_cmb)
            wait("vector", sX, h2d)
            ch_dv = {}
            for p in range(8):
                if p >= 2:
                    wait("tensor", sDV, ch_dv[p - 2])
                pe_p = 0
                for bi, (b0, bw) in enumerate(blocks):
                    off = b0 - base
                    pb = Pp[(p % 2) if bi == mainbi else 2 + (p % 2)]
                    pe_p = MM(pb[0:128, 0:bw], sb_pairsel[:, p * 128:(p + 1) * 128],
                              a_cmb[0:16, off:off + bw], True, True, inc=True)
                wait("vector", sPE, pe_p)
                for bi, (b0, bw) in enumerate(blocks):
                    off = b0 - base
                    pb = Pp[(p % 2) if bi == mainbi else 2 + (p % 2)]
                    opq("vector", lambda e, p=p, off=off, bw=bw, pb=pb: e.tensor_tensor(
                        a_ch[p][:, off:off + bw], a_h2[:, off:off + bw], pb[0:128, 0:bw], OP.mult))
                ch_dv[p] = now("vector")
            wait("tensor", sDV, ch_dv[7])

            if _FF < 5:
                continue
            ev_dv = {-2: 0, -1: 0}
            for mq in range(KC):
                pmain = Pp[2 * (mq % 2)]
                pov = Pp[2 * (mq % 2) + 1]
                wait("tensor", sDV, ev_dv[mq - 2])
                first = True
                for pc in range(8):
                    piece, cv = ring_load_c("base", mq * 8 + pc,
                                            d_base[mq, :, pc * 512:(pc + 1) * 512],
                                            d_cb["base"][mq, :, pc * 512:(pc + 1) * 512],
                                            512, scale=0.5)
                    wait("tensor", cv)
                    for kk in range(4):
                        k = pc * 4 + kk
                        lhs = piece[:, kk * 128:(kk + 1) * 128]
                        for bi, (b0, bw) in enumerate(blocks):
                            off = b0 - base
                            MM((pmain if bi == mainbi else pov)[0:128, 0:bw], lhs,
                               a_ln[:, k * 516 + off: k * 516 + off + bw], first, False,
                               inc=(kk == 3 and bi == len(blocks) - 1))
                        first = False
                    ring_done()
                piece, cv = ring_load_c("bexp", mq * 2, d_bexp[mq, :, 0:512],
                                        d_cb["bexp"][mq, :, 0:512], 512)
                piece2, cv2 = ring_load_c("bexp", mq * 2 + 1, d_bexp[mq, :, 512:1024],
                                          d_cb["bexp"][mq, :, 512:1024], 512)
                wait("tensor", cv)
                wait("tensor", cv2)
                pe_mq = 0
                for p in range(8):
                    src = piece if p < 4 else piece2
                    lhs = src[:, (p % 4) * 128:(p % 4) * 128 + 128]
                    for bi, (b0, bw) in enumerate(blocks):
                        off = b0 - base
                        pe_mq = MM((pmain if bi == mainbi else pov)[0:128, 0:bw], lhs,
                                   a_ch[p][:, off:off + bw], False, p == 7,
                                   inc=(p == 7 and bi == len(blocks) - 1))
                ring_done()
                ring_done()
                wait("vector", sPE, pe_mq)
                for bi, (b0, bw) in enumerate(blocks):
                    ps = pmain if bi == mainbi else pov
                    opq("vector", lambda e, mq=mq, b0=b0, bw=bw, ps=ps: e.scalar_tensor_tensor(
                        sb_xT[:, mq * W + b0: mq * W + b0 + bw], ps[0:128, 0:bw],
                        sb_bbh[:, mq:mq + 1], sb_xT[:, mq * W + b0: mq * W + b0 + bw],
                        OP.add, OP.add))
                ev_dv[mq] = now("vector")
            barrier()

    PT32 = PT.bitcast(F32)

    def qkv_conv():
        for g in range(2):
            base = 512 * g
            blocks = [(base, 4), (base + 4, 512)]
            apply_dv, psum_free = layer_norm(base, 516, blocks)
            wait("tensor", sDV, apply_dv[-1])
            wait("tensor", sDV, psum_free)

            qk_ac = {-2: 0, -1: 0}
            for mq in range(12):
                pmain = Pp[2 * (mq % 2)]
                pov = Pp[2 * (mq % 2) + 1]
                wait("tensor", sAC, qk_ac[mq - 2])
                first = True
                pe_mq = 0
                for pc in range(8):
                    piece, cv = ring_load_c("qkv", mq * 8 + pc,
                                            d_wqkv[mq, :, pc * 512:(pc + 1) * 512],
                                            d_cb["qkv"][mq, :, pc * 512:(pc + 1) * 512], 512,
                                            eng="scalar")
                    wait("tensor", cv)
                    for kk in range(4):
                        k = pc * 4 + kk
                        lhs = piece[:, kk * 128:(kk + 1) * 128]
                        for bi, (b0, bw) in enumerate(blocks):
                            off = b0 - base
                            pe_mq = MM((pmain if bi == 1 else pov)[0:128, 0:bw], lhs,
                                       a_ln[:, k * 516 + off: k * 516 + off + bw],
                                       first, k == KC - 1,
                                       inc=(kk == 3 and bi == 1))
                        first = False
                    ring_done()
                wait("scalar", sPE, pe_mq)
                for bi, (b0, bw) in enumerate(blocks):
                    off = b0 - base
                    ps = pmain if bi == 1 else pov
                    opq("scalar", lambda e, mq=mq, off=off, bw=bw, ps=ps: e.activation(
                        a_qkv[:, mq * 516 + off: mq * 516 + off + bw], ps[0:128, 0:bw], AF.Copy))
                qk_ac[mq] = now("scalar")
            wait("vector", sAC, qk_ac[11])
            if g == 0:
                for mq in range(12):
                    opq("vector", lambda e, mq=mq: e.tensor_scalar(
                        a_qkv[:, mq * 516: mq * 516 + 4], a_qkv[:, mq * 516: mq * 516 + 4],
                        sb_hscale[:, 0:1], None, OP.mult))
            hs_dv = now("vector")
            wait("tensor", sAC, qk_ac[11])
            wait("tensor", sDV, hs_dv)

            for j in range(8):
                MM(Pp[5][0:64, 0:512], sb_sumhead[:, :],
                   a_qkv[:, j * 516 + 4: j * 516 + 516], j == 0, j == 7, inc=(j == 7))
            pe_mn = 0
            for j in range(2):
                pe_mn = MM(Pp[6][0:64, 0:512], sb_sumhead[:, :],
                           a_qkv[:, (8 + j) * 516 + 4: (8 + j) * 516 + 516], j == 0, j == 1,
                           inc=(j == 1))
            wait("scalar", sPE, pe_mn)
            opq("scalar", lambda e: e.activation(a_means[0:64, 0:512], Pp[5][0:64, 0:512],
                                                 AF.Copy, scale=1.0 / 32.0))
            ac_mn = opq("scalar", lambda e: e.activation(
                a_means[0:64, 512:1024], Pp[6][0:64, 0:512], AF.Copy, scale=0.125))
            wait("gpsimd", sAC, ac_mn)
            mn_dup = dma("gpsimd", a_means[64:128, :], a_means[0:64, :])
            wait("gpsimd", sX, mn_dup)
            wait("vector", sX, mn_dup)
            wait("sync", sPE, pe_mn)
            wait("sync", sW, sW.v)
            dma("sync", a_ropec[:, 0:512], d_cos[:, g * 512:(g + 1) * 512])
            rp_ld = dma("sync", a_ropes[:, 0:512], d_sin[:, g * 512:(g + 1) * 512])
            wait("vector", sW, rp_ld)

            vt_dv = {-2: 0, -1: 0}
            vt_pub = {-2: 0, -1: 0, -3: 0}
            for i in range(8):
                tt, vh = i // 2, i % 2
                src_col = (10 + vh) * 516 + (4 - vh) + tt * 128
                sl = i % 2
                if i >= 2:
                    wait("tensor", sDV, vt_dv[i - 2])
                pe_t = opq("tensor", lambda e, src_col=src_col, sl=sl: e.transpose(
                    PT[0:128, sl * 128: sl * 128 + 128],
                    a_qkv[:, src_col: src_col + 128], sb_ident[:, :]))
                wait("vector", sPE, pe_t)
                wait("vector", sX, vt_pub[i - 2])
                opq("vector", lambda e, sl=sl: e.tensor_copy(
                    a_vt[:, sl * 132: sl * 132 + 64], PT[0:128, sl * 128: sl * 128 + 64]))
                opq("vector", lambda e, sl=sl: e.memset(a_vt[:, sl * 132 + 64: sl * 132 + 65], 1.0))
                opq("vector", lambda e, sl=sl: e.tensor_copy(
                    a_vt[:, sl * 132 + 66: sl * 132 + 66 + 64],
                    PT[0:128, sl * 128 + 64: sl * 128 + 128]))
                vt_dv[i] = opq("vector", lambda e, sl=sl: e.memset(
                    a_vt[:, sl * 132 + 66 + 64: sl * 132 + 66 + 65], 1.0))
                wait("gpsimd", sDV, vt_dv[i])
                wait("gpsimd", sX, vt_pub[i - 1])
                d_t = d_a1inA if g == 0 else d_a1inB
                for h in range(2):
                    kv = 2 * vh + h
                    for c in (2 * kv, 2 * kv + 1):
                        dma("gpsimd", d_t[c, :, 1024 + tt * 65: 1024 + tt * 65 + 65],
                            a_vt[:, sl * 132 + h * 66: sl * 132 + h * 66 + 65])
                vt_pub[i] = sX.v

            load_const_bf(a_kc1w, d_kc1, 2 * 3 * 64)
            kcw_dv = load_const_bf(a_kc2w, d_kc2, 2 * 2 * 3 * 128)
            wait("tensor", sDV, kcw_dv)
            k1_parts = []
            for j in range(2):
                pa = Pp[2 * (j % 2)]
                pb2 = Pp[2 * (j % 2) + 1]
                pe_k1 = 0
                for g2 in range(2):
                    rb = 64 * g2
                    for tap in range(3):
                        lhs = a_kc1w[rb:rb + 64, (j * 3 + tap) * 64: (j * 3 + tap) * 64 + 64]
                        rhs_c = (8 + j) * 516 + tap
                        MM(pa[rb:rb + 64, 0:512],
                           lhs, a_qkv[rb:rb + 64, rhs_c + 2: rhs_c + 2 + 512],
                           tap == 0, tap == 2, tp=(rb, rb))
                        pe_k1 = MM(pb2[rb:rb + 64, 0:2],
                                   lhs, a_qkv[rb:rb + 64, rhs_c: rhs_c + 2],
                                   tap == 0, tap == 2, tp=(rb, rb),
                                   inc=(g2 == 1 and tap == 2))
                wait("vector", sPE, pe_k1)
                opq("vector", lambda e, j=j, pa=pa: e.tensor_scalar(
                    a_k1[:, j * 514 + 2: j * 514 + 2 + 512], pa[0:128, 0:512],
                    sb_kc1bias[:, j:j + 1], None, OP.add))
                opq("vector", lambda e, j=j, pb2=pb2: e.tensor_scalar(
                    a_k1[:, j * 514: j * 514 + 2], pb2[0:128, 0:2],
                    sb_kc1bias[:, j:j + 1], None, OP.add))
            if g == 0:
                selfsync("vector")
                for j in range(2):
                    opq("vector", lambda e, j=j: e.tensor_scalar(
                        a_k1[:, j * 514: j * 514 + 2], a_k1[:, j * 514: j * 514 + 2],
                        sb_hscale[:, 0:1], None, OP.mult))
            k1_dv = now("vector")
            wait("tensor", sDV, k1_dv)

            kf_pub = 0
            prev_dv = 0
            prev_ac = 0
            for mq in range(2):
                pc2 = Pp[4 + (mq % 2)]
                wait("tensor", sDV, prev_dv)
                wait("tensor", sAC, prev_ac)
                first = True
                pe_c2 = 0
                for kq in range(2):
                    for tap in range(3):
                        lhs = a_kc2w[:, ((kq * 2 + mq) * 3 + tap) * 128:
                                     ((kq * 2 + mq) * 3 + tap) * 128 + 128]
                        pe_c2 = MM(pc2[0:128, 0:512], lhs,
                                   a_k1[:, kq * 514 + tap: kq * 514 + tap + 512],
                                   first, kq == 1 and tap == 2,
                                   inc=(kq == 1 and tap == 2))
                        first = False
                wait("vector", sPE, pe_c2)
                cvs = a_cv[:, (mq % 2) * 512:(mq % 2) * 512 + 512]
                opq("vector", lambda e, mq=mq, pc2=pc2, cvs=cvs: e.scalar_tensor_tensor(
                    cvs, pc2[0:128, 0:512], sb_kc2bias[:, mq:mq + 1],
                    a_means[:, 0:512], OP.add, OP.add))
                selfsync("vector")
                dv_c2 = opq("vector", lambda e, cvs=cvs: e.tensor_tensor(
                    a_sq[:, 0:512], cvs, cvs, OP.mult))
                wait("tensor", sDV, dv_c2)
                MM(Pp[6][0:2, 0:512], sb_ones2[:, :], a_sq[:, 0:512], True, True, inc=True)
                pe_sw = MM(Pp[2][0:128, 0:512], sb_perm[:, :], cvs, True, True, inc=True)
                wait("scalar", sPE, pe_sw)
                opq("scalar", lambda e: e.activation(a_rr[0:2, 0:512], Pp[6][0:2, 0:512], AF.Sqrt))
                selfsync("scalar")
                opq("scalar", lambda e: e.activation(a_rr[0:2, 0:512], a_rr[0:2, 0:512], AF.Ln))
                selfsync("scalar")
                ac_rr = opq("scalar", lambda e: e.activation(
                    a_rr[0:2, 0:512], a_rr[0:2, 0:512], AF.Exp, scale=-1.0))
                wait("vector", sAC, ac_rr)
                dv_kt = opq("vector", lambda e: e.tensor_scalar(
                    a_rr[0:2, 0:512], a_rr[0:2, 0:512], sb_ktcol[0:2, 0:1], None, OP.mult))
                wait("gpsimd", sDV, dv_kt)
                bcast(a_rr[0:1, 0:512], 512, [(a_rstdb[0:64, 0:512], 64)])
                bc2 = bcast(a_rr[1:2, 0:512], 512, [(a_rstdb[64:128, 0:512], 64)])
                wait("vector", sX, bc2)
                wait("vector", sPE, pe_sw)
                opq("vector", lambda e, cvs=cvs: e.tensor_tensor(
                    a_t1[:, 0:512], cvs, a_ropec[:, 0:512], OP.mult))
                opq("vector", lambda e: e.tensor_tensor(
                    a_t2[:, 0:512], Pp[2][0:128, 0:512], a_ropes[:, 0:512], OP.mult))
                selfsync("vector")
                opq("vector", lambda e: e.tensor_tensor(
                    a_t1[:, 0:512], a_t1[:, 0:512], a_t2[:, 0:512], OP.add))
                wait("vector", sX, kf_pub)
                selfsync("vector")
                dv_kf = opq("vector", lambda e: e.tensor_tensor(
                    a_qf[:, 0:512], a_t1[:, 0:512], a_rstdb[:, 0:512], OP.mult))
                wait("gpsimd", sDV, dv_kf)
                d_t = d_a1inA if g == 0 else d_a1inB
                for h in range(2):
                    kv = 2 * mq + h
                    for c in (2 * kv, 2 * kv + 1):
                        dma("gpsimd", d_t[c, 0:64, 512:1024],
                            a_qf[64 * h:64 * h + 64, 0:512])
                        dma("gpsimd", d_t[c, 64:128, 512:1024],
                            a_qf[64 * h:64 * h + 64, 0:512])
                wait("gpsimd", sX, sX.v)
                kf_pub = sX.v
                prev_dv = dv_kf
                prev_ac = ac_rr

            qcw_dv = load_const_bf(a_qc1w, d_qc1, 8 * 3 * 64)
            wait("tensor", sDV, qcw_dv)
            q1_ac = {-2: 0, -1: 0}
            for j in range(8):
                pa = Pp[2 * (j % 2)]
                pb2 = Pp[2 * (j % 2) + 1]
                wait("tensor", sDV, q1_ac[j - 2])
                pe_q1 = 0
                for g2 in range(2):
                    rb = 64 * g2
                    for tap in range(3):
                        lhs = a_qc1w[rb:rb + 64, (j * 3 + tap) * 64: (j * 3 + tap) * 64 + 64]
                        rhs_c = j * 516 + tap
                        MM(pa[rb:rb + 64, 0:512],
                           lhs, a_qkv[rb:rb + 64, rhs_c + 2: rhs_c + 2 + 512],
                           tap == 0, tap == 2, tp=(rb, rb))
                        pe_q1 = MM(pb2[rb:rb + 64, 0:2],
                                   lhs, a_qkv[rb:rb + 64, rhs_c: rhs_c + 2],
                                   tap == 0, tap == 2, tp=(rb, rb),
                                   inc=(g2 == 1 and tap == 2))
                wait("vector", sPE, pe_q1)
                opq("vector", lambda e, j=j, pa=pa: e.tensor_scalar(
                    a_q1[:, j * 514 + 2: j * 514 + 2 + 512], pa[0:128, 0:512],
                    sb_qc1bias[:, j:j + 1], None, OP.add))
                opq("vector", lambda e, j=j, pb2=pb2: e.tensor_scalar(
                    a_q1[:, j * 514: j * 514 + 2], pb2[0:128, 0:2],
                    sb_qc1bias[:, j:j + 1], None, OP.add))
                q1_ac[j] = now("vector")
            if g == 0:
                selfsync("vector")
                for j in range(8):
                    opq("vector", lambda e, j=j: e.tensor_scalar(
                        a_q1[:, j * 514: j * 514 + 2], a_q1[:, j * 514: j * 514 + 2],
                        sb_hscale[:, 0:1], None, OP.mult))
            q1_dv = now("vector")
            wait("tensor", sDV, q1_dv)

            qf_pub = kf_pub
            prev_dv = 0
            prev_ac = 0
            for mq in range(8):
                pc2 = Pp[4 + (mq % 2)]
                wait("tensor", sDV, prev_dv)
                wait("tensor", sAC, prev_ac)
                first = True
                pe_c2 = 0
                for pc in range(6):
                    piece, cv = ring_load_c("qc2", mq * 6 + pc,
                                            d_qc2[mq, :, pc * 512:(pc + 1) * 512],
                                            d_cb["qc2"][mq, :, pc * 512:(pc + 1) * 512], 512,
                                            eng="scalar")
                    wait("tensor", cv)
                    for ii in range(4):
                        idx = pc * 4 + ii
                        kq, tap = idx // 3, idx % 3
                        lhs = piece[:, ii * 128: ii * 128 + 128]
                        pe_c2 = MM(pc2[0:128, 0:512], lhs,
                                   a_q1[:, kq * 514 + tap: kq * 514 + tap + 512], first, idx == 23,
                                   inc=(ii == 3 or idx == 23))
                        first = False
                    ring_done()
                wait("vector", sPE, pe_c2)
                cvs = a_cv[:, (mq % 2) * 512:(mq % 2) * 512 + 512]
                opq("vector", lambda e, mq=mq, pc2=pc2, cvs=cvs: e.scalar_tensor_tensor(
                    cvs, pc2[0:128, 0:512], sb_qc2bias[:, mq:mq + 1],
                    a_means[:, 512:1024], OP.add, OP.add))
                selfsync("vector")
                dv_c2 = opq("vector", lambda e, cvs=cvs: e.tensor_tensor(
                    a_sq[:, 0:512], cvs, cvs, OP.mult))
                wait("tensor", sDV, dv_c2)
                MM(Pp[6][0:2, 0:512], sb_ones2[:, :], a_sq[:, 0:512], True, True, inc=True)
                pe_sw = MM(Pp[2][0:128, 0:512], sb_perm[:, :], cvs, True, True, inc=True)
                wait("scalar", sPE, pe_sw)
                opq("scalar", lambda e: e.activation(a_rr[0:2, 0:512], Pp[6][0:2, 0:512],
                                                     AF.Sqrt, scale=64.0))
                selfsync("scalar")
                opq("scalar", lambda e: e.activation(a_rr[0:2, 0:512], a_rr[0:2, 0:512], AF.Ln))
                selfsync("scalar")
                ac_rr = opq("scalar", lambda e: e.activation(
                    a_rr[0:2, 0:512], a_rr[0:2, 0:512], AF.Exp, scale=-1.0))
                wait("gpsimd", sAC, ac_rr)
                bcast(a_rr[0:1, 0:512], 512, [(a_rstdb[0:64, 0:512], 64)])
                bc2 = bcast(a_rr[1:2, 0:512], 512, [(a_rstdb[64:128, 0:512], 64)])
                wait("vector", sX, bc2)
                wait("vector", sPE, pe_sw)
                opq("vector", lambda e, cvs=cvs: e.tensor_tensor(
                    a_t1[:, 0:512], cvs, a_ropec[:, 0:512], OP.mult))
                opq("vector", lambda e: e.tensor_tensor(
                    a_t2[:, 0:512], Pp[2][0:128, 0:512], a_ropes[:, 0:512], OP.mult))
                selfsync("vector")
                opq("vector", lambda e: e.tensor_tensor(
                    a_t1[:, 0:512], a_t1[:, 0:512], a_t2[:, 0:512], OP.add))
                wait("vector", sX, qf_pub)
                selfsync("vector")
                dv_qf = opq("vector", lambda e: e.tensor_tensor(
                    a_qf[:, 0:512], a_t1[:, 0:512], a_rstdb[:, 0:512], OP.mult))
                wait("gpsimd", sDV, dv_qf)
                dma("gpsimd", (d_a1inA if g == 0 else d_a1inB)[mq, :, 0:512], a_qf[:, 0:512])
                wait("gpsimd", sX, sX.v)
                qf_pub = sX.v
                prev_dv = dv_qf
                prev_ac = ac_rr
            barrier()
            op("gpsimd", sCC, lambda e, g=g: e.collective_compute(
                "AllToAll", OP.bypass, replica_groups=[list(range(8))],
                ins=[(d_a1inA if g == 0 else d_a1inB)[:, :, :]],
                outs=[(d_a1outA if g == 0 else d_a1outB)[:, :, :]]))

    def attention():
        wait("sync", sW, sW.v)
        msk_ld = dma("sync", t_masks[:, :], d_masks[:, :])
        wait("vector", sW, msk_ld)
        wait("gpsimd", sCC, sCC.v)
        b_pe_done = 0
        ao_dv = {0: 0, 1: 0}
        ao_ac = {0: 0, 1: 0}
        ao_pub = [0]
        dn_x = [0]
        for b in range(2):
            wait("gpsimd", sPE, b_pe_done)
            asm_t = 0
            for qt in range(4):
                p = b * 4 + qt
                for gg, dsrc in ((0, d_a1outA), (1, d_a1outB)):
                    dma("gpsimd", t_Q[:, qt * 1024 + gg * 512: qt * 1024 + gg * 512 + 512],
                        dsrc[p, :, 0:512])
                    dma("gpsimd", t_K[:, qt * 1024 + gg * 512: qt * 1024 + gg * 512 + 512],
                        dsrc[p, :, 512:1024])
                    asm_t = dma(
                        "gpsimd",
                        t_VT.rearrange("p (t c) -> p t c", c=66)[:, qt * 8 + gg * 4: qt * 8 + gg * 4 + 4, 0:65],
                        dsrc[p, :, 1024:1284].rearrange("p (t c) -> p t c", c=65))
            wait("gpsimd", sX, asm_t)
            wait("tensor", sX, asm_t)
            exp_t = {}
            msk_t = {}
            av_t = {-2: 0, -1: 0}
            for qg in range(NQG):
                nkt = 4 * qg + 4
                AVa = Pp[4] if qg % 2 == 0 else Pp[6]
                AVb = Pp[5] if qg % 2 == 0 else PT32

                def scores(kt, qg=qg):
                    pA = Pp[2 * (kt % 2)]
                    pB = Pp[2 * (kt % 2) + 1]
                    MM(pA[0:128, 0:512], t_K[0:64, kt * 128: kt * 128 + 128],
                       t_Q[0:64, qg * 512: qg * 512 + 512], True, True, tp=(0, 0), inc=True)
                    sc = MM(pB[0:128, 0:512], t_K[64:128, kt * 128: kt * 128 + 128],
                            t_Q[64:128, qg * 512: qg * 512 + 512], True, True, tp=(64, 0), inc=True)
                    wait("scalar", sPE, sc)
                    wait("scalar", sPE, av_t[kt - 2])
                    opq("scalar", lambda e, kt=kt, pA=pA: e.activation(
                        t_at[:, (2 * kt % 4) * 512: (2 * kt % 4) * 512 + 512],
                        pA[0:128, 0:512], AF.Exp, bias=sb_expbias[:, 0:1]))
                    exp_t[kt] = opq("scalar", lambda e, kt=kt, pB=pB: e.activation(
                        t_at[:, ((2 * kt + 1) % 4) * 512: ((2 * kt + 1) % 4) * 512 + 512],
                        pB[0:128, 0:512], AF.Exp, bias=sb_expbias[:, 0:1]))
                    if kt >= 4 * qg:
                        j = kt - 4 * qg
                        wait("vector", sAC, exp_t[kt])
                        for s2 in (2 * kt % 4, (2 * kt + 1) % 4):
                            opq("vector", lambda e, s2=s2, j=j: e.tensor_tensor(
                                t_at[:, s2 * 512: s2 * 512 + 512],
                                t_at[:, s2 * 512: s2 * 512 + 512],
                                t_masks[:, j * 512: j * 512 + 512], OP.mult))
                        msk_t[kt] = now("vector")

                def av(kt, qg=qg, nkt=nkt, AVa=AVa, AVb=AVb):
                    if kt >= 4 * qg:
                        wait("tensor", sDV, msk_t[kt])
                    else:
                        wait("tensor", sAC, exp_t[kt])
                    if kt == 0:
                        wait("tensor", sDV, ao_dv[qg % 2])
                        wait("tensor", sAC, ao_ac[qg % 2])
                    MM(AVa[0:65, 0:512], t_VT[:, kt * 66: kt * 66 + 65],
                       t_at[:, (2 * kt % 4) * 512: (2 * kt % 4) * 512 + 512],
                       kt == 0, kt == nkt - 1, inc=True)
                    av_t[kt] = MM(AVb[0:65, 0:512], t_VT[:, kt * 66: kt * 66 + 65],
                                  t_at[:, ((2 * kt + 1) % 4) * 512: ((2 * kt + 1) % 4) * 512 + 512],
                                  kt == 0, kt == nkt - 1, inc=True)

                scores(0)
                for kt in range(1, nkt):
                    scores(kt)
                    av(kt - 1)
                av(nkt - 1)
                pe_fin = av_t[nkt - 1]
                wait("scalar", sPE, pe_fin)
                wait("scalar", sX, dn_x[0])
                opq("scalar", lambda e, AVa=AVa: e.activation(
                    t_dn[64:65, 0:512], AVa[64:65, 0:512], AF.Ln))
                opq("scalar", lambda e, AVb=AVb: e.activation(
                    t_dn[64:65, 512:1024], AVb[64:65, 0:512], AF.Ln))
                selfsync("scalar")
                opq("scalar", lambda e: e.activation(t_dn[64:65, 0:512], t_dn[64:65, 0:512],
                                                     AF.Exp, scale=-1.0))
                ac_dn = opq("scalar", lambda e: e.activation(
                    t_dn[64:65, 512:1024], t_dn[64:65, 512:1024], AF.Exp, scale=-1.0))
                ao_ac[qg % 2] = ac_dn
                wait("gpsimd", sAC, ac_dn)
                wait("gpsimd", sDV, ao_dv[(qg + 1) % 2])
                sl = bc_i[0] % 8
                bc_i[0] += 1
                t1b = dma("gpsimd", d_bc[sl:sl + 1, 0:1024].bitcast(BF16)[:, 0:1024] if False
                          else d_bc[sl:sl + 1, 0:1024], t_dn[64:65, 0:1024])
                wait("gpsimd", sX, t1b)
                dma("gpsimd", t_rdb[0:64, 0:512], d_bc[sl, 0:512].partition_broadcast(64))
                dma("gpsimd", t_rdb[0:64, 512:1024], d_bc[sl, 512:1024].partition_broadcast(64))
                wait("gpsimd", sX, sX.v)
                dn_x[0] = sX.v
                wait("vector", sX, dn_x[0])
                wait("vector", sPE, pe_fin)
                wait("vector", sX, ao_pub[0])
                opq("vector", lambda e, AVa=AVa: e.tensor_tensor(
                    t_ao[0:64, 0:512], AVa[0:64, 0:512], t_rdb[0:64, 0:512], OP.mult))
                opq("vector", lambda e, AVb=AVb: e.tensor_tensor(
                    t_ao[0:64, 512:1024], AVb[0:64, 0:512], t_rdb[0:64, 512:1024], OP.mult))
                ao_dv[qg % 2] = now("vector")
                wait("gpsimd", sDV, ao_dv[qg % 2])
                pdst = b * 4 + qg // 2
                cb = (qg % 2) * 512
                dma("gpsimd", d_a2in[pdst, 0:64, cb:cb + 512], t_ao[0:64, 0:512])
                ao_pub[0] = dma("gpsimd", d_a2in[pdst, 64:128, cb:cb + 512], t_ao[0:64, 512:1024])
                wait("gpsimd", sX, ao_pub[0])
            b_pe_done = now("tensor")
        wait("gpsimd", sX, sX.v)
        cc2 = op("gpsimd", sCC, lambda e: e.collective_compute(
            "AllToAll", OP.bypass, replica_groups=[list(range(8))],
            ins=[d_a2in[:, :, :]], outs=[d_a2out[:, :, :]]))
        wait("gpsimd", sCC, cc2)
        barrier()

    def wo():
        for blk in range(2):
            asm_t = 0
            for c in range(8):
                asm_t = dma("gpsimd", a_af[:, c * 512:(c + 1) * 512],
                            d_a2out[c, :, blk * 512: blk * 512 + 512])
            wait("gpsimd", sX, asm_t)
            wait("tensor", sX, asm_t)
            ev_dv = {-2: 0, -1: 0}
            for mq in range(KC):
                ps = Pp[2 * (mq % 2)]
                wait("tensor", sDV, ev_dv[mq - 2])
                first = True
                pe_mq = 0
                for pc in range(2):
                    piece, cv = ring_load_c("wo", mq * 2 + pc,
                                            d_wo[mq, :, pc * 512:(pc + 1) * 512],
                                            d_cb["wo"][mq, :, pc * 512:(pc + 1) * 512], 512)
                    wait("tensor", cv)
                    for kk in range(4):
                        kq = pc * 4 + kk
                        pe_mq = MM(ps[0:128, 0:512], piece[:, kk * 128: kk * 128 + 128],
                                   a_af[:, kq * 512:(kq + 1) * 512], first, kq == 7,
                                   inc=(kk == 3))
                        first = False
                    ring_done()
                wait("vector", sPE, pe_mq)
                col = mq * W + HALO + blk * 512
                opq("vector", lambda e, col=col, ps=ps: e.tensor_tensor(
                    sb_xT[:, col: col + 512], sb_xT[:, col: col + 512], ps[0:128, 0:512], OP.add))
                ev_dv[mq] = now("vector")
            barrier()

    if _PHASES >= 1:
        ffn(0)
    if _PHASES >= 2:
        qkv_conv()
        attention()
        wo()
    if _PHASES >= 3:
        ffn(1)

    dv_all = now("vector")
    wait("gpsimd", sDV, dv_all)
    for k in range(KC):
        dma("gpsimd", d_yT[k * 128:(k + 1) * 128, :], sb_xT[:, k * W + HALO:(k + 1) * W], ctr=sST)
    wait("gpsimd", sST, sST.v)

    with nc.Block() as block:
        @block.sync
        def _(e):
            for f in EV["sync"]:
                f(e)

        @block.tensor
        def _(e):
            for f in EV["tensor"]:
                f(e)

        @block.vector
        def _(e):
            for f in EV["vector"]:
                f(e)

        @block.scalar
        def _(e):
            for f in EV["scalar"]:
                f(e)

        @block.gpsimd
        def _(e):
            for f in EV["gpsimd"]:
                f(e)

    return nc


_prog = None


def _get_prog():
    global _prog
    if _prog is None:
        _prog = build()
    return _prog


_consts = None


def _get_consts():
    global _consts
    if _consts is not None:
        return _consts
    ident = np.eye(128, dtype=BF)
    perm = np.zeros((128, 128), dtype=BF)
    for p2 in range(128):
        perm[p2, p2 ^ 1] = 1.0
    sumhead = np.zeros((128, 64), dtype=BF)
    for d in range(64):
        sumhead[d, d] = 1.0
        sumhead[d + 64, d] = 1.0
    ones1 = np.ones((128, 1), dtype=BF)
    ones2 = np.zeros((128, 2), dtype=BF)
    ones2[0:64, 0] = 1.0
    ones2[64:128, 1] = 1.0
    pairsel = np.zeros((16, 8 * 128), dtype=BF)
    for p2 in range(8):
        for d in range(128):
            pairsel[2 * p2 + d // 64, p2 * 128 + d] = 1.0
    masks = np.zeros((128, 4 * 512), dtype=BF)
    for j in range(4):
        for kk in range(128):
            masks[kk, j * 512 + 128 * j + kk: (j + 1) * 512] = 1.0
    oner64 = np.zeros((128, 64), dtype=BF)
    oner64[64, :] = 1.0
    _consts = dict(ident_bf=ident, perm_bf=perm, sumhead_bf=sumhead, ones1_bf=ones1,
                   ones2_bf=ones2, pairsel_bf=pairsel, masks_bf=masks, oner64_bf=oner64)
    return _consts


def _block_weights(inp):
    wq = np.ascontiguousarray(
        inp["L0_attn_w_qkv"].reshape(12, 128, 32, 128).transpose(0, 3, 2, 1)
    ).reshape(12, 128, KC * 128)
    bw = np.ascontiguousarray(
        inp["L0_ffn_base_w"].reshape(32, 128, 32, 128).transpose(0, 3, 2, 1)
    ).reshape(KC, 128, KC * 128)
    wo = np.ascontiguousarray(
        inp["L0_attn_w_o"].reshape(32, 128, 8, 128).transpose(0, 3, 2, 1)
    ).reshape(KC, 128, 8 * 128)
    be = np.ascontiguousarray(
        inp["L0_ffn_B"].transpose(0, 2, 1).reshape(8, 128, 4096).reshape(8, 128, 32, 128)
        .transpose(2, 1, 0, 3)
    ).reshape(KC, 128, 8 * 128)
    qc2 = np.ascontiguousarray(
        inp["L0_attn_qc2_w"].reshape(8, 128, 8, 128, 3).transpose(0, 3, 2, 4, 1)
    ).reshape(8, 128, 8 * 3 * 128)
    ar = np.zeros((128, KC * 128), np.float32)
    AR = np.concatenate([inp["L0_ffn_A"], inp["L0_ffn_router_w"]], axis=0)
    art = AR.T.reshape(32, 128, 80)
    ar3 = ar.reshape(128, 32, 128)
    ar3[:, :, 0:80] = art.transpose(1, 0, 2)
    qc1 = np.ascontiguousarray(
        inp["L0_attn_qc1_w"].reshape(8, 2, 64, 64, 3).transpose(1, 3, 0, 4, 2)
    ).reshape(128, 8 * 3 * 64)
    kc1 = np.ascontiguousarray(
        inp["L0_attn_kc1_w"].reshape(2, 2, 64, 64, 3).transpose(1, 3, 0, 4, 2)
    ).reshape(128, 2 * 3 * 64)
    kc2 = np.ascontiguousarray(
        inp["L0_attn_kc2_w"].reshape(2, 128, 2, 128, 3).transpose(3, 2, 0, 4, 1)
    ).reshape(128, 2 * 2 * 3 * 128)
    return dict(
        wqkv_b=wq, base_b=bw, wo_b=wo, bexp_b=be, qc2_b=qc2, ar_b=ar,
        qc1_b=qc1, kc1_b=kc1, kc2_b=kc2,
        qc1_bias=np.ascontiguousarray(inp["L0_attn_qc1_b"].reshape(8, 128).T),
        kc1_bias=np.ascontiguousarray(inp["L0_attn_kc1_b"].reshape(2, 128).T),
        qc2_bias=np.ascontiguousarray(inp["L0_attn_qc2_b"].reshape(8, 128).T),
        kc2_bias=np.ascontiguousarray(inp["L0_attn_kc2_b"].reshape(2, 128).T),
        base_bias=np.ascontiguousarray(inp["L0_ffn_base_b"].reshape(32, 128).T),
        key_temp=np.asarray(inp["L0_attn_key_temp"], np.float32).reshape(1),
    )


def _rope_tables(qstart):
    inv = (1.0 / 10000.0 ** (np.arange(0, HD, 2, dtype=np.float64) / HD))
    t = np.arange(qstart, qstart + TOKQ, dtype=np.float64)
    ang = t[None, :] * inv[:, None]
    cos = np.cos(ang)
    sin = np.sin(ang)
    cos128 = np.zeros((128, TOKQ), np.float32)
    sin128 = np.zeros((128, TOKQ), np.float32)
    for r in range(128):
        i = (r % 64) // 2
        cos128[r] = cos[i]
        sin128[r] = sin[i] if (r % 2 == 1) else -sin[i]
    return cos128, sin128


def kernel(**inputs):
    inp = {k: np.asarray(v) for k, v in inputs.items()}
    x = inp["L0_attn_x"].astype(np.float32)
    wmaps = _block_weights(inp)
    consts = _get_consts()
    in_maps = []
    for c in range(8):
        b, qt = c // 4, c % 4
        qstart = qt * TOKQ
        xs = np.zeros((D, W), np.float32)
        lo = max(0, qstart - HALO)
        nh = qstart - lo
        if nh > 0:
            xs[:, HALO - nh:HALO] = x[b, lo:qstart, :].T
        xs[:, HALO:] = x[b, qstart:qstart + TOKQ, :].T
        cos128, sin128 = _rope_tables(qstart)
        m = dict(wmaps)
        m.update(consts)
        m["xT"] = xs
        m["rope_cos"] = cos128
        m["rope_sin"] = sin128
        m["halo_scale"] = np.array([0.0 if qt == 0 else 1.0], np.float32)
        in_maps.append(m)
    nc = _get_prog()
    res = run_bass_kernel_spmd(nc, in_maps, core_ids=list(range(8)))
    out = np.empty((2, S, D), np.float32)
    for c in range(8):
        b, qt = c // 4, c % 4
        out[b, qt * TOKQ:(qt + 1) * TOKQ, :] = res.results[c]["yT"].T
    return out


# version 11
# speedup vs baseline: 1.0618x; 1.0618x over previous
import os
os.environ.setdefault("TRN_ALLOW_ICI_ALL_TO_ALL", "1")

import numpy as np
import ml_dtypes
import concourse.bass as bass
import concourse.mybir as mybir
from concourse.bass_utils import run_bass_kernel_spmd

F32 = mybir.dt.float32
BF16 = mybir.dt.bfloat16
AF = mybir.ActivationFunctionType
OP = mybir.AluOpType
AX = mybir.AxisListType
BF = ml_dtypes.bfloat16

D = 4096
KC = 32
TOKQ = 1024
HALO = 4
W = TOKQ + HALO
NH, NKV, HD = 16, 4, 64
LAT, KVD = 1024, 256
S = 4096
EPS = 1e-5
NQG, NKT = 8, 32

SH1_Q, SH1_K, SH1_V = 0, 1024, 2048
SH1C = 2568

_PHASES = int(os.environ.get("MK_PHASES", "3"))
_FF = int(os.environ.get("MK_FF", "5"))
_LN = int(os.environ.get("MK_LN", "4"))


def build():
    nc = bass.Bass("TRN2", target_bir_lowering=False)

    d_xT = nc.dram_tensor("xT", [D, W], F32, kind="ExternalInput").ap()
    d_base = nc.dram_tensor("base_b", [KC, 128, KC * 128], F32, kind="ExternalInput").ap()
    d_bexp = nc.dram_tensor("bexp_b", [KC, 128, 8 * 128], F32, kind="ExternalInput").ap()
    d_ar = nc.dram_tensor("ar_b", [128, KC * 128], F32, kind="ExternalInput").ap()
    d_wqkv = nc.dram_tensor("wqkv_b", [12, 128, KC * 128], F32, kind="ExternalInput").ap()
    d_wo = nc.dram_tensor("wo_b", [KC, 128, 8 * 128], F32, kind="ExternalInput").ap()
    d_qc2 = nc.dram_tensor("qc2_b", [8, 128, 8 * 3 * 128], F32, kind="ExternalInput").ap()
    d_qc1 = nc.dram_tensor("qc1_b", [128, 8 * 3 * 64], F32, kind="ExternalInput").ap()
    d_kc1 = nc.dram_tensor("kc1_b", [128, 2 * 3 * 64], F32, kind="ExternalInput").ap()
    d_kc2 = nc.dram_tensor("kc2_b", [128, 2 * 2 * 3 * 128], F32, kind="ExternalInput").ap()
    d_qc1bias = nc.dram_tensor("qc1_bias", [128, 8], F32, kind="ExternalInput").ap()
    d_kc1bias = nc.dram_tensor("kc1_bias", [128, 2], F32, kind="ExternalInput").ap()
    d_qc2bias = nc.dram_tensor("qc2_bias", [128, 8], F32, kind="ExternalInput").ap()
    d_kc2bias = nc.dram_tensor("kc2_bias", [128, 2], F32, kind="ExternalInput").ap()
    d_bbias = nc.dram_tensor("base_bias", [128, 32], F32, kind="ExternalInput").ap()
    d_ktemp = nc.dram_tensor("key_temp", [1], F32, kind="ExternalInput").ap()
    d_ident = nc.dram_tensor("ident_bf", [128, 128], BF16, kind="ExternalInput").ap()
    d_perm = nc.dram_tensor("perm_bf", [128, 128], BF16, kind="ExternalInput").ap()
    d_sumhead = nc.dram_tensor("sumhead_bf", [128, 64], BF16, kind="ExternalInput").ap()
    d_ones1 = nc.dram_tensor("ones1_bf", [128, 1], BF16, kind="ExternalInput").ap()
    d_ones2 = nc.dram_tensor("ones2_bf", [128, 2], BF16, kind="ExternalInput").ap()
    d_pairsel = nc.dram_tensor("pairsel_bf", [16, 8 * 128], BF16, kind="ExternalInput").ap()
    d_oner64 = nc.dram_tensor("oner64_bf", [128, 64], BF16, kind="ExternalInput").ap()
    d_masks = nc.dram_tensor("masks_bf", [128, 4 * 512], BF16, kind="ExternalInput").ap()
    d_cos = nc.dram_tensor("rope_cos", [128, TOKQ], F32, kind="ExternalInput").ap()
    d_sin = nc.dram_tensor("rope_sin", [128, TOKQ], F32, kind="ExternalInput").ap()
    d_hscale = nc.dram_tensor("halo_scale", [1], F32, kind="ExternalInput").ap()

    d_yT = nc.dram_tensor("yT", [D, TOKQ], F32, kind="ExternalOutput").ap()

    d_bc = nc.dram_tensor("bc_scratch", [8, 1040], F32).ap()
    d_cb = {
        "base": nc.dram_tensor("c_base", [KC, 128, KC * 128], BF16).ap(),
        "bexp": nc.dram_tensor("c_bexp", [KC, 128, 8 * 128], BF16).ap(),
        "ar": nc.dram_tensor("c_ar", [128, KC * 128], BF16).ap(),
        "qkv": nc.dram_tensor("c_wqkv", [12, 128, KC * 128], BF16).ap(),
        "qc2": nc.dram_tensor("c_qc2", [8, 128, 8 * 3 * 128], BF16).ap(),
        "wo": nc.dram_tensor("c_wo", [KC, 128, 8 * 128], BF16).ap(),
    }
    d_a1inA = nc.dram_tensor("a2a1_inA", [8, 128, 1284], BF16).ap()
    d_a1outA = nc.dram_tensor("a2a1_outA", [8, 128, 1284], BF16).ap()
    d_a1inB = nc.dram_tensor("a2a1_inB", [8, 128, 1284], BF16).ap()
    d_a1outB = nc.dram_tensor("a2a1_outB", [8, 128, 1284], BF16).ap()
    d_a2in = nc.dram_tensor("a2a2_in", [8, 128, TOKQ], BF16).ap()
    d_a2out = nc.dram_tensor("a2a2_out", [8, 128, TOKQ], BF16).ap()

    sb_xT = nc.alloc_sbuf_tensor("sb_xT", [128, KC * W], F32).ap()
    sb_stage = nc.alloc_sbuf_tensor("sb_stage", [128, 2 * 512], F32).ap()
    sb_wbf = nc.alloc_sbuf_tensor("sb_wbf", [128, 8 * 512], BF16).ap()
    sb_ident = nc.alloc_sbuf_tensor("sb_ident", [128, 128], BF16).ap()
    sb_perm = nc.alloc_sbuf_tensor("sb_perm", [128, 128], BF16).ap()
    sb_sumhead = nc.alloc_sbuf_tensor("sb_sumhead", [128, 64], BF16).ap()
    sb_ones1 = nc.alloc_sbuf_tensor("sb_ones1", [128, 1], BF16).ap()
    sb_ones2 = nc.alloc_sbuf_tensor("sb_ones2", [128, 2], BF16).ap()
    sb_pairsel = nc.alloc_sbuf_tensor("sb_pairsel", [16, 8 * 128], BF16).ap()
    sb_oner64 = nc.alloc_sbuf_tensor("sb_oner64", [128, 64], BF16).ap()
    sb_qc1bias = nc.alloc_sbuf_tensor("sb_qc1bias", [128, 8], F32).ap()
    sb_kc1bias = nc.alloc_sbuf_tensor("sb_kc1bias", [128, 2], F32).ap()
    sb_qc2bias = nc.alloc_sbuf_tensor("sb_qc2bias", [128, 8], F32).ap()
    sb_kc2bias = nc.alloc_sbuf_tensor("sb_kc2bias", [128, 2], F32).ap()
    sb_bbh = nc.alloc_sbuf_tensor("sb_bbh", [128, 32], F32).ap()
    sb_ktcol = nc.alloc_sbuf_tensor("sb_ktcol", [128, 1], F32).ap()
    sb_expbias = nc.alloc_sbuf_tensor("sb_expbias", [128, 1], F32).ap()
    sb_hscale = nc.alloc_sbuf_tensor("sb_hscale", [128, 1], F32).ap()
    sb_eps = nc.alloc_sbuf_tensor("sb_eps", [128, 1], F32).ap()

    ARN = 29456
    sb_arena = nc.alloc_sbuf_tensor("sb_arena", [128, ARN], BF16).ap()

    Pp = [nc.alloc_psum_tensor(f"p{i}", [128, 512], F32).ap() for i in range(7)]
    PT = nc.alloc_psum_tensor("pt", [128, 1024], BF16).ap()

    def abf(off, cols):
        return sb_arena[:, off:off + cols]

    def af32(off, cols):
        return sb_arena[:, off:off + 2 * cols].bitcast(F32)

    a_ln = abf(0, KC * 516)
    o = 0
    a_means = abf(o, 1024); o += 1024
    a_t1 = af32(o, 512); o += 1024
    a_t2 = af32(o, 512); o += 1024
    a_sq = abf(o, 512); o += 512
    a_rr = af32(o, 512); o += 1024
    a_rstdb = af32(o, 512); o += 1024
    a_cv = abf(o, 2 * 512); o += 1024
    a_qf = abf(o, 512); o += 512
    a_vt = abf(o, 2 * 132); o += 264
    a_ropec = af32(o, 512); o += 1024
    a_ropes = af32(o, 512); o += 1024
    OV_SH = o
    o = OV_SH
    a_k1 = abf(o, 2 * 514); o += 1028
    a_kc1w = abf(o, 2 * 3 * 64); o += 384
    a_kc2w = abf(o, 2 * 2 * 3 * 128); o += 1536
    assert o <= 16512, o
    o = OV_SH
    a_q1 = abf(o, 8 * 514); o += 8 * 514
    a_qc1w = abf(o, 8 * 3 * 64); o += 1536
    assert o <= 16512, o

    o = 16512
    a_xbf = abf(o, 2 * 520); o += 1040
    a_x2 = abf(o, 520); o += 520
    a_abc = af32(o, 516); o += 1032
    a_bbc = af32(o, 516); o += 1032
    a_rows = af32(o, 1032); o += 2064
    a_lnt = af32(o, 516); o += 1032
    a_cs = af32(o, 8); o += 16
    assert o <= 23256, o

    C0 = 23256
    a_qkv = abf(C0, 12 * 516)
    o = C0
    a_h2 = abf(o, 516); o += 528
    a_lg = abf(o, 516); o += 528
    a_lt = abf(o, 96); o += 96
    a_le = abf(o, 96); o += 96
    a_cmb = abf(o, 516); o += 528
    CH0 = o
    a_ch = [abf(CH0 + i * 516, 516) for i in range(8)]
    o += 8 * 516
    assert o <= C0 + 6192, o
    a_af = abf(C0, 8 * 512)

    o = 0
    t_K = abf(o, S); o += S
    t_Q = abf(o, S); o += S
    t_VT = abf(o, NKT * 66); o += NKT * 66
    t_masks = abf(o, 4 * 512); o += 2048
    t_at = abf(o, 4 * 512); o += 2048
    t_ao = abf(o, 2 * 512); o += 1024
    t_dn = af32(o, 1024); o += 2048
    t_rdb = af32(o, 1024); o += 2048
    t_dnh = abf(o, 1024); o += 1024
    t_dnl = abf(o, 1024); o += 1024
    assert o <= ARN, o

    EV = {"sync": [], "tensor": [], "vector": [], "scalar": [], "gpsimd": []}
    ENGS = list(EV)

    def emit(eng, f):
        EV[eng].append(f)

    class Ctr:
        def __init__(s, name):
            s.h = nc.alloc_semaphore(name=name)
            s.v = 0

        def inc(s, n=1):
            s.v += n
            return s.v

    sW, sCV = Ctr("sW"), Ctr("sCV")
    sX, sPE, sDV, sAC = Ctr("sX"), Ctr("sPE"), Ctr("sDV"), Ctr("sAC")
    sCC, sST = Ctr("sCC"), Ctr("sST")
    sR = [Ctr(f"sR{i}") for i in range(8)]

    ECTR = {"tensor": sPE, "vector": sDV, "scalar": sAC, "gpsimd": sX, "sync": sW}
    ALLC = [sW, sCV, sX, sPE, sDV, sAC, sCC, sST] + sR

    def wait(eng, ctr, tgt=None):
        if tgt is None:
            ctr, tgt = ctr
        if tgt <= 0:
            return
        emit(eng, lambda e, h=ctr.h, t=tgt: e.wait_ge(h, t))

    def barrier():
        for e in ENGS:
            for c in ALLC:
                wait(e, c, c.v)

    def dma(eng, out_ap, in_ap, ctr=None):
        c = ctr if ctr is not None else (sW if eng == "sync" else sX)
        c.inc(16)
        emit(eng, lambda e, oa=out_ap, ia=in_ap, h=c.h: e.dma_start(out=oa, in_=ia).then_inc(h, 16))
        return c.v

    def op(eng, ctr, f):
        ctr.inc(1)
        emit(eng, lambda e, f=f, h=ctr.h: f(e).then_inc(h, 1))
        return ctr.v

    def opq(eng, f):
        return op(eng, ECTR[eng], f)

    def MM(out, lhsT, rhs, start, stop, tp=None, inc=False):
        if inc:
            sPE.inc(1)
            emit("tensor", lambda e, oo=out, l=lhsT, r=rhs, a=start, b=stop, t=tp, h=sPE.h:
                 e.matmul(oo, l, r, start=a, stop=b, tile_position=t).then_inc(h, 1))
        else:
            emit("tensor", lambda e, oo=out, l=lhsT, r=rhs, a=start, b=stop, t=tp:
                 e.matmul(oo, l, r, start=a, stop=b, tile_position=t))
        return sPE.v

    def now(eng):
        return ECTR[eng].v

    def selfsync(eng):
        c = ECTR[eng]
        wait(eng, c, c.v)

    bc_i = [0]

    def bcast(row_ap, n, outs):
        sl = bc_i[0] % 8
        bc_i[0] += 1
        t1 = dma("gpsimd", d_bc[sl:sl + 1, 0:n], row_ap)
        wait("gpsimd", sX, t1)
        for out_ap, cnt in outs:
            dma("gpsimd", out_ap, d_bc[sl, 0:n].partition_broadcast(cnt))
        wait("gpsimd", sX, sX.v)
        return sX.v

    ring_i = [0]
    ring_fin = {}
    ring_fp = [0]
    _cached = set()
    _stage_last = {0: None, 1: None}

    def ring_load(piece_ap, ncols, scale=1.0, eng="vector"):
        i = ring_i[0]
        ring_i[0] += 1
        st, wb = i % 2, i % 8
        if _stage_last[st] is not None:
            wait("sync", *_stage_last[st])
        tgt = dma("sync", sb_stage[:, st * 512: st * 512 + ncols], piece_ap, ctr=sR[wb])
        wait(eng, sR[wb], tgt)
        wait(eng, sCV, sCV.v)
        if i >= 8:
            for ctr, val in ring_fin[i - 8]:
                wait(eng, ctr, val)
        if eng == "vector":
            cv = op("vector", sCV, lambda e, st=st, wb=wb, nn=ncols, sc=scale: e.tensor_scalar(
                sb_wbf[:, wb * 512: wb * 512 + nn], sb_stage[:, st * 512: st * 512 + nn],
                sc, None, OP.mult))
        else:
            cv = op("scalar", sCV, lambda e, st=st, wb=wb, nn=ncols, sc=scale: e.activation(
                sb_wbf[:, wb * 512: wb * 512 + nn], sb_stage[:, st * 512: st * 512 + nn],
                AF.Copy, scale=sc))
        _stage_last[st] = (sCV, cv)
        return sb_wbf[:, wb * 512:(wb + 1) * 512], cv

    ring_extra = {}

    def ring_done(eng="tensor"):
        fins = [(ECTR[eng], ECTR[eng].v)]
        if ring_fp[0] in ring_extra:
            fins.append(ring_extra.pop(ring_fp[0]))
        ring_fin[ring_fp[0]] = fins
        ring_fp[0] += 1

    def ring_load_c(key, pidx, f32_piece, bf_piece, ncols, scale=1.0, eng="vector"):
        i = ring_i[0]
        ring_i[0] += 1
        st, wb = i % 2, i % 8
        wslot = sb_wbf[:, wb * 512: wb * 512 + ncols]
        if (key, pidx) not in _cached:
            _cached.add((key, pidx))
            if _stage_last[st] is not None:
                wait("sync", *_stage_last[st])
            tgt = dma("sync", sb_stage[:, st * 512: st * 512 + ncols], f32_piece, ctr=sR[wb])
            wait(eng, sR[wb], tgt)
            wait(eng, sCV, sCV.v)
            if i >= 8:
                for ctr, val in ring_fin[i - 8]:
                    wait(eng, ctr, val)
            if eng == "vector":
                cv = op("vector", sCV, lambda e, w=wslot, st=st, nn=ncols, sc=scale: e.tensor_scalar(
                    w, sb_stage[:, st * 512: st * 512 + nn], sc, None, OP.mult))
            else:
                cv = op("scalar", sCV, lambda e, w=wslot, st=st, nn=ncols, sc=scale: e.activation(
                    w, sb_stage[:, st * 512: st * 512 + nn], AF.Copy, scale=sc))
            _stage_last[st] = (sCV, cv)
            wait("gpsimd", sCV, cv)
            wbv = dma("gpsimd", bf_piece, wslot)
            wait("gpsimd", sX, wbv)
            ring_extra[i] = (sX, wbv)
            return wslot, (sCV, cv)
        if i >= 8:
            for ctr, val in ring_fin[i - 8]:
                wait("sync", ctr, val)
        tgt = dma("sync", wslot, bf_piece, ctr=sR[wb])
        return wslot, (sR[wb], tgt)

    def load_const_bf(dst_bf, dram_ap, total):
        offc = 0
        while offc < total:
            n = min(512, total - offc)
            piece, cv = ring_load(dram_ap[:, offc:offc + n], n)
            wait("vector", sCV, cv)
            opq("vector", lambda e, p=piece, dd=dst_bf, oo=offc, n=n: e.tensor_copy(
                dd[:, oo:oo + n], p[:, 0:n]))
            ring_done("vector")
            offc += n
        return now("vector")

    for k in range(KC):
        dma("gpsimd", sb_xT[:, k * W:(k + 1) * W], d_xT[k * 128:(k + 1) * 128, :])
    dma("sync", sb_ident[:, :], d_ident[:, :])
    dma("sync", sb_perm[:, :], d_perm[:, :])
    dma("sync", sb_sumhead[:, :], d_sumhead[:, :])
    dma("sync", sb_ones1[:, :], d_ones1[:, :])
    dma("sync", sb_ones2[:, :], d_ones2[:, :])
    dma("sync", sb_pairsel[:, :], d_pairsel[:, :])
    dma("sync", sb_oner64[:, :], d_oner64[:, :])
    dma("sync", sb_qc1bias[:, :], d_qc1bias[:, :])
    dma("sync", sb_kc1bias[:, :], d_kc1bias[:, :])
    dma("sync", sb_qc2bias[:, :], d_qc2bias[:, :])
    dma("sync", sb_kc2bias[:, :], d_kc2bias[:, :])
    dma("sync", sb_bbh[:, :], d_bbias[:, :])
    dma("sync", sb_ktcol[:, :], d_ktemp[0:1].partition_broadcast(128))
    t_ld = dma("sync", sb_hscale[:, :], d_hscale[0:1].partition_broadcast(128))
    wait("vector", sW, t_ld)
    opq("vector", lambda e: e.tensor_scalar(sb_bbh[:, :], sb_bbh[:, :], 0.5, None, OP.mult))
    opq("vector", lambda e: e.memset(sb_eps[:, :], EPS))
    wait("scalar", sW, t_ld)
    opq("scalar", lambda e: e.activation(sb_expbias[:, :], sb_ktcol[:, :], AF.Abs))
    selfsync("scalar")
    opq("scalar", lambda e: e.mul(sb_expbias[:, :], sb_expbias[:, :], -0.125))
    barrier()

    def layer_norm(base, cw, blocks):
        pe_slot = {-2: 0, -1: 0}
        for k in range(KC):
            sl = k % 2
            xb = a_xbf[:, sl * 520: sl * 520 + cw]
            x2 = a_x2[:, 0:cw]
            wait("vector", sPE, pe_slot[k - 2])
            cvt = opq("vector", lambda e, k=k, xb=xb: e.tensor_copy(
                xb, sb_xT[:, k * W + base: k * W + base + cw]))
            wait("scalar", sDV, cvt)
            wait("scalar", sPE, pe_slot[k - 1])
            sqt = opq("scalar", lambda e, xb=xb, x2=x2: e.activation(x2, xb, AF.Square))
            wait("tensor", sDV, cvt)
            wait("tensor", sAC, sqt)
            for bi, (b0, bw) in enumerate(blocks):
                off = b0 - base
                MM(Pp[2 * bi][0:1, 0:bw], sb_ones1[:, :], xb[:, off:off + bw], k == 0, k == KC - 1)
                MM(Pp[2 * bi + 1][0:1, 0:bw], sb_ones1[:, :], x2[:, off:off + bw],
                   k == 0, k == KC - 1, inc=(bi == len(blocks) - 1))
            pe_slot[k] = sPE.v
        pe_done = pe_slot[KC - 1]
        mrow = a_rows[0:1, 0:516]
        vrow = a_rows[0:1, 516:1032]
        trow = a_lnt[0:1, 0:516]
        lo = blocks[0][0] - base
        hi = blocks[-1][0] + blocks[-1][1] - base
        nb = hi - lo
        wait("vector", sPE, pe_done)
        wait("vector", sX, sX.v)
        for bi, (b0, bw) in enumerate(blocks):
            off = b0 - base
            opq("vector", lambda e, bi=bi, off=off, bw=bw: e.tensor_scalar(
                mrow[:, off:off + bw], Pp[2 * bi][0:1, 0:bw], 1.0 / D, None, OP.mult))
            opq("vector", lambda e, bi=bi, off=off, bw=bw: e.tensor_scalar(
                vrow[:, off:off + bw], Pp[2 * bi + 1][0:1, 0:bw], 1.0 / D, None, OP.mult))
        psum_free = now("vector")
        selfsync("vector")
        opq("vector", lambda e, lo=lo, nb=nb: e.tensor_tensor(
            trow[:, lo:lo + nb], mrow[:, lo:lo + nb], mrow[:, lo:lo + nb], OP.mult))
        selfsync("vector")
        dvv = opq("vector", lambda e, lo=lo, nb=nb: e.tensor_tensor(
            vrow[:, lo:lo + nb], vrow[:, lo:lo + nb], trow[:, lo:lo + nb], OP.subtract))
        wait("scalar", sDV, dvv)
        opq("scalar", lambda e, lo=lo, nb=nb: e.activation(
            vrow[:, lo:lo + nb], vrow[:, lo:lo + nb], AF.Sqrt, bias=sb_eps[0:1, 0:1]))
        selfsync("scalar")
        opq("scalar", lambda e, lo=lo, nb=nb: e.activation(
            vrow[:, lo:lo + nb], vrow[:, lo:lo + nb], AF.Ln))
        selfsync("scalar")
        sct = opq("scalar", lambda e, lo=lo, nb=nb: e.activation(
            vrow[:, lo:lo + nb], vrow[:, lo:lo + nb], AF.Exp, scale=-1.0))
        wait("vector", sAC, sct)
        opq("vector", lambda e, lo=lo, nb=nb: e.tensor_tensor(
            mrow[:, lo:lo + nb], mrow[:, lo:lo + nb], vrow[:, lo:lo + nb], OP.mult))
        selfsync("vector")
        dvr = opq("vector", lambda e, lo=lo, nb=nb: e.tensor_scalar(
            mrow[:, lo:lo + nb], mrow[:, lo:lo + nb], -1.0, None, OP.mult))
        wait("gpsimd", sDV, dvr)
        sl = bc_i[0] % 8
        bc_i[0] += 1
        t1 = dma("gpsimd", d_bc[sl:sl + 1, 0:1032], a_rows[0:1, 0:1032])
        wait("gpsimd", sX, t1)
        dma("gpsimd", a_abc[:, lo:lo + nb], d_bc[sl, 516 + lo:516 + hi].partition_broadcast(128))
        dma("gpsimd", a_bbc[:, lo:lo + nb], d_bc[sl, lo:hi].partition_broadcast(128))
        wait("gpsimd", sX, sX.v)
        bc_done = sX.v
        wait("vector", sX, bc_done)
        apply_dv = []
        for k in range(KC):
            for (b0, bw) in blocks:
                off = b0 - base
                selfsync("vector")
                opq("vector", lambda e, k=k, b0=b0, off=off, bw=bw: e.tensor_tensor(
                    a_lnt[:, off:off + bw], sb_xT[:, k * W + b0: k * W + b0 + bw],
                    a_abc[:, off:off + bw], OP.mult))
                selfsync("vector")
                opq("vector", lambda e, k=k, off=off, bw=bw: e.tensor_tensor(
                    a_ln[:, k * 516 + off: k * 516 + off + bw], a_lnt[:, off:off + bw],
                    a_bbc[:, off:off + bw], OP.add))
            apply_dv.append(now("vector"))
        return apply_dv, psum_free

    def ffn(call):
        for g in range(2):
            if g == 0:
                base, cw = 0, 516
                blocks = ([(0, 4), (4, 512)] if call == 0 else [(4, 512)])
            else:
                base, cw = 516, 512
                blocks = [(516, 512)]
            mainbi = len(blocks) - 1
            apply_dv, psum_free = layer_norm(base, cw, blocks)
            if _FF < 2:
                continue

            for pc in range(8):
                piece, cv = ring_load_c("ar", pc, d_ar[:, pc * 512:(pc + 1) * 512],
                                        d_cb["ar"][:, pc * 512:(pc + 1) * 512], 512, eng="scalar")
                wait("tensor", cv)
                wait("tensor", sDV, apply_dv[pc * 4 + 3])
                for kk in range(4):
                    k = pc * 4 + kk
                    for bi, (b0, bw) in enumerate(blocks):
                        off = b0 - base
                        MM(Pp[4 + (bi != mainbi)][0:80, 0:bw], piece[:, kk * 128: kk * 128 + 80],
                           a_ln[:, k * 516 + off: k * 516 + off + bw], k == 0, k == KC - 1,
                           inc=(kk == 3 and bi == len(blocks) - 1))
                ring_done()
            pe_ar = now("tensor")

            wait("scalar", sPE, pe_ar)
            for bi, (b0, bw) in enumerate(blocks):
                off = b0 - base
                pb = Pp[4 + (bi != mainbi)]
                opq("scalar", lambda e, pb=pb, off=off, bw=bw: e.activation(
                    a_h2[0:64, off:off + bw], pb[0:64, 0:bw], AF.Copy))
                opq("scalar", lambda e, pb=pb, off=off, bw=bw: e.activation(
                    a_lg[64:80, off:off + bw], pb[64:80, 0:bw], AF.Copy))
            ac_ev = now("scalar")
            wait("gpsimd", sAC, ac_ev)
            o0 = blocks[0][0] - base
            ow = (blocks[-1][0] + blocks[-1][1]) - blocks[0][0]
            h2d = dma("gpsimd", a_h2[64:128, o0:o0 + ow], a_h2[0:64, o0:o0 + ow])
            wait("gpsimd", sX, h2d)

            if _FF < 3:
                continue
            nblk = (ow + 127) // 128
            wait("tensor", sAC, ac_ev)
            for t in range(nblk):
                tw = min(128, ow - t * 128)
                opq("tensor", lambda e, t=t, tw=tw, o0=o0: e.transpose(
                    PT[0:tw, t * 16: t * 16 + 16],
                    a_lg[64:80, o0 + t * 128: o0 + t * 128 + tw], sb_ident[64:80, 64:80]))
            pe_tr = now("tensor")
            wait("vector", sPE, pe_tr)
            for t in range(nblk):
                tw = min(128, ow - t * 128)
                opq("vector", lambda e, t=t, tw=tw: e.tensor_copy(
                    a_lt[0:tw, t * 16: t * 16 + 16], PT[0:tw, t * 16: t * 16 + 16]))
            dv_lt = now("vector")
            for t in range(nblk):
                tw = min(128, ow - t * 128)
                lt = a_lt[0:tw, t * 16: t * 16 + 16]
                le = a_le[0:tw, t * 16: t * 16 + 16]
                m0 = a_cs[0:tw, 0:1]; zz = a_cs[0:tw, 1:2]; m1 = a_cs[0:tw, 2:3]
                m2 = a_cs[0:tw, 3:4]; dd = a_cs[0:tw, 4:5]; w1 = a_cs[0:tw, 5:6]; w2 = a_cs[0:tw, 6:7]
                selfsync("vector")
                opq("vector", lambda e, lt=lt, m0=m0: e.tensor_reduce(m0, lt, AX.X, OP.max))
                selfsync("vector")
                dv1 = opq("vector", lambda e, lt=lt, m0=m0: e.tensor_scalar(
                    lt, lt, m0, None, OP.subtract))
                wait("scalar", sDV, dv1)
                ac1 = opq("scalar", lambda e, lt=lt, le=le, zz=zz: e.activation(
                    le, lt, AF.Exp, accum_out=zz))
                wait("vector", sAC, ac1)
                opq("vector", lambda e, le=le, m1=m1: e.tensor_reduce(m1, le, AX.X, OP.max))
                selfsync("vector")
                opq("vector", lambda e, le=le, m1=m1, lt=lt: e.tensor_scalar(
                    lt, le, m1, None, OP.is_equal))
                selfsync("vector")
                opq("vector", lambda e, le=le, lt=lt: e.tensor_tensor(lt, lt, le, OP.mult))
                selfsync("vector")
                opq("vector", lambda e, le=le, lt=lt: e.tensor_tensor(lt, le, lt, OP.subtract))
                selfsync("vector")
                opq("vector", lambda e, lt=lt, m2=m2: e.tensor_reduce(m2, lt, AX.X, OP.max))
                opq("vector", lambda e, zz=zz, dd=dd: e.tensor_scalar(dd, zz, 1e-6, None, OP.mult))
                selfsync("vector")
                opq("vector", lambda e, dd=dd, m1=m1: e.tensor_tensor(dd, dd, m1, OP.add))
                selfsync("vector")
                opq("vector", lambda e, dd=dd, m2=m2: e.tensor_tensor(dd, dd, m2, OP.add))
                selfsync("vector")
                opq("vector", lambda e, dd=dd: e.reciprocal(dd, dd))
                selfsync("vector")
                opq("vector", lambda e, dd=dd: e.tensor_scalar(dd, dd, 0.25, None, OP.mult))
                selfsync("vector")
                opq("vector", lambda e, m1=m1, dd=dd, w1=w1: e.tensor_tensor(w1, m1, dd, OP.mult))
                opq("vector", lambda e, m2=m2, dd=dd, w2=w2: e.tensor_tensor(w2, m2, dd, OP.mult))
                selfsync("vector")
                opq("vector", lambda e, le=le, m1=m1, lt=lt: e.tensor_scalar(
                    lt, le, m1, None, OP.is_equal))
                selfsync("vector")
                opq("vector", lambda e, lt=lt, w1=w1: e.tensor_scalar(lt, lt, w1, None, OP.mult))
                opq("vector", lambda e, le=le, m2=m2: e.tensor_scalar(le, le, m2, None, OP.is_equal))
                selfsync("vector")
                opq("vector", lambda e, le=le, w2=w2: e.tensor_scalar(le, le, w2, None, OP.mult))
                selfsync("vector")
                opq("vector", lambda e, lt=lt, le=le: e.tensor_tensor(le, lt, le, OP.add))
            dv_tk = now("vector")
            wait("tensor", sDV, dv_tk)
            for t in range(nblk):
                tw = min(128, ow - t * 128)
                opq("tensor", lambda e, t=t, tw=tw: e.transpose(
                    PT[0:16, t * 128: t * 128 + tw],
                    a_le[0:tw, t * 16: t * 16 + 16], sb_ident[0:tw, 0:tw]))
            pe_tc = now("tensor")
            wait("scalar", sPE, pe_tc)
            for t in range(nblk):
                tw = min(128, ow - t * 128)
                opq("scalar", lambda e, t=t, tw=tw, o0=o0: e.activation(
                    a_cmb[0:16, o0 + t * 128: o0 + t * 128 + tw],
                    PT[0:16, t * 128: t * 128 + tw], AF.Copy))
            ac_cmb = now("scalar")

            if _FF < 4:
                continue
            wait("tensor", sAC, ac_cmb)
            wait("tensor", sX, h2d)
            wait("tensor", sDV, psum_free)
            wait("vector", sAC, ac_cmb)
            wait("vector", sX, h2d)
            ch_dv = {}
            for p in range(8):
                if p >= 2:
                    wait("tensor", sDV, ch_dv[p - 2])
                pe_p = 0
                for bi, (b0, bw) in enumerate(blocks):
                    off = b0 - base
                    pb = Pp[(p % 2) if bi == mainbi else 2 + (p % 2)]
                    pe_p = MM(pb[0:128, 0:bw], sb_pairsel[:, p * 128:(p + 1) * 128],
                              a_cmb[0:16, off:off + bw], True, True, inc=True)
                wait("vector", sPE, pe_p)
                for bi, (b0, bw) in enumerate(blocks):
                    off = b0 - base
                    pb = Pp[(p % 2) if bi == mainbi else 2 + (p % 2)]
                    opq("vector", lambda e, p=p, off=off, bw=bw, pb=pb: e.tensor_tensor(
                        a_ch[p][:, off:off + bw], a_h2[:, off:off + bw], pb[0:128, 0:bw], OP.mult))
                ch_dv[p] = now("vector")
            wait("tensor", sDV, ch_dv[7])

            if _FF < 5:
                continue
            ev_dv = {-2: 0, -1: 0}
            for mq in range(KC):
                pmain = Pp[2 * (mq % 2)]
                pov = Pp[2 * (mq % 2) + 1]
                wait("tensor", sDV, ev_dv[mq - 2])
                first = True
                for pc in range(8):
                    piece, cv = ring_load_c("base", mq * 8 + pc,
                                            d_base[mq, :, pc * 512:(pc + 1) * 512],
                                            d_cb["base"][mq, :, pc * 512:(pc + 1) * 512],
                                            512, scale=0.5)
                    wait("tensor", cv)
                    for kk in range(4):
                        k = pc * 4 + kk
                        lhs = piece[:, kk * 128:(kk + 1) * 128]
                        for bi, (b0, bw) in enumerate(blocks):
                            off = b0 - base
                            MM((pmain if bi == mainbi else pov)[0:128, 0:bw], lhs,
                               a_ln[:, k * 516 + off: k * 516 + off + bw], first, False,
                               inc=(kk == 3 and bi == len(blocks) - 1))
                        first = False
                    ring_done()
                piece, cv = ring_load_c("bexp", mq * 2, d_bexp[mq, :, 0:512],
                                        d_cb["bexp"][mq, :, 0:512], 512)
                piece2, cv2 = ring_load_c("bexp", mq * 2 + 1, d_bexp[mq, :, 512:1024],
                                          d_cb["bexp"][mq, :, 512:1024], 512)
                wait("tensor", cv)
                wait("tensor", cv2)
                pe_mq = 0
                for p in range(8):
                    src = piece if p < 4 else piece2
                    lhs = src[:, (p % 4) * 128:(p % 4) * 128 + 128]
                    for bi, (b0, bw) in enumerate(blocks):
                        off = b0 - base
                        pe_mq = MM((pmain if bi == mainbi else pov)[0:128, 0:bw], lhs,
                                   a_ch[p][:, off:off + bw], False, p == 7,
                                   inc=(p == 7 and bi == len(blocks) - 1))
                ring_done()
                ring_done()
                wait("vector", sPE, pe_mq)
                for bi, (b0, bw) in enumerate(blocks):
                    ps = pmain if bi == mainbi else pov
                    opq("vector", lambda e, mq=mq, b0=b0, bw=bw, ps=ps: e.scalar_tensor_tensor(
                        sb_xT[:, mq * W + b0: mq * W + b0 + bw], ps[0:128, 0:bw],
                        sb_bbh[:, mq:mq + 1], sb_xT[:, mq * W + b0: mq * W + b0 + bw],
                        OP.add, OP.add))
                ev_dv[mq] = now("vector")
            barrier()

    PT32 = PT.bitcast(F32)

    def qkv_conv():
        for g in range(2):
            base = 512 * g
            blocks = [(base, 4), (base + 4, 512)]
            apply_dv, psum_free = layer_norm(base, 516, blocks)
            wait("tensor", sDV, apply_dv[-1])
            wait("tensor", sDV, psum_free)

            qk_ac = {-2: 0, -1: 0}
            for mq in range(12):
                pmain = Pp[2 * (mq % 2)]
                pov = Pp[2 * (mq % 2) + 1]
                wait("tensor", sAC, qk_ac[mq - 2])
                first = True
                pe_mq = 0
                for pc in range(8):
                    piece, cv = ring_load_c("qkv", mq * 8 + pc,
                                            d_wqkv[mq, :, pc * 512:(pc + 1) * 512],
                                            d_cb["qkv"][mq, :, pc * 512:(pc + 1) * 512], 512,
                                            eng="scalar")
                    wait("tensor", cv)
                    for kk in range(4):
                        k = pc * 4 + kk
                        lhs = piece[:, kk * 128:(kk + 1) * 128]
                        for bi, (b0, bw) in enumerate(blocks):
                            off = b0 - base
                            pe_mq = MM((pmain if bi == 1 else pov)[0:128, 0:bw], lhs,
                                       a_ln[:, k * 516 + off: k * 516 + off + bw],
                                       first, k == KC - 1,
                                       inc=(kk == 3 and bi == 1))
                        first = False
                    ring_done()
                wait("scalar", sPE, pe_mq)
                for bi, (b0, bw) in enumerate(blocks):
                    off = b0 - base
                    ps = pmain if bi == 1 else pov
                    opq("scalar", lambda e, mq=mq, off=off, bw=bw, ps=ps: e.activation(
                        a_qkv[:, mq * 516 + off: mq * 516 + off + bw], ps[0:128, 0:bw], AF.Copy))
                qk_ac[mq] = now("scalar")
            wait("vector", sAC, qk_ac[11])
            if g == 0:
                for mq in range(12):
                    opq("vector", lambda e, mq=mq: e.tensor_scalar(
                        a_qkv[:, mq * 516: mq * 516 + 4], a_qkv[:, mq * 516: mq * 516 + 4],
                        sb_hscale[:, 0:1], None, OP.mult))
            hs_dv = now("vector")
            wait("tensor", sAC, qk_ac[11])
            wait("tensor", sDV, hs_dv)

            for j in range(8):
                MM(Pp[5][0:64, 0:512], sb_sumhead[:, :],
                   a_qkv[:, j * 516 + 4: j * 516 + 516], j == 0, j == 7, inc=(j == 7))
            pe_mn = 0
            for j in range(2):
                pe_mn = MM(Pp[6][0:64, 0:512], sb_sumhead[:, :],
                           a_qkv[:, (8 + j) * 516 + 4: (8 + j) * 516 + 516], j == 0, j == 1,
                           inc=(j == 1))
            wait("scalar", sPE, pe_mn)
            opq("scalar", lambda e: e.activation(a_means[0:64, 0:512], Pp[5][0:64, 0:512],
                                                 AF.Copy, scale=1.0 / 32.0))
            ac_mn = opq("scalar", lambda e: e.activation(
                a_means[0:64, 512:1024], Pp[6][0:64, 0:512], AF.Copy, scale=0.125))
            wait("gpsimd", sAC, ac_mn)
            mn_dup = dma("gpsimd", a_means[64:128, :], a_means[0:64, :])
            wait("gpsimd", sX, mn_dup)
            wait("vector", sX, mn_dup)
            wait("sync", sPE, pe_mn)
            wait("sync", sW, sW.v)
            dma("sync", a_ropec[:, 0:512], d_cos[:, g * 512:(g + 1) * 512])
            rp_ld = dma("sync", a_ropes[:, 0:512], d_sin[:, g * 512:(g + 1) * 512])
            wait("vector", sW, rp_ld)

            vt_dv = {-2: 0, -1: 0}
            vt_pub = {-2: 0, -1: 0, -3: 0}
            for i in range(8):
                tt, vh = i // 2, i % 2
                src_col = (10 + vh) * 516 + (4 - vh) + tt * 128
                sl = i % 2
                if i >= 2:
                    wait("tensor", sDV, vt_dv[i - 2])
                pe_t = opq("tensor", lambda e, src_col=src_col, sl=sl: e.transpose(
                    PT[0:128, sl * 128: sl * 128 + 128],
                    a_qkv[:, src_col: src_col + 128], sb_ident[:, :]))
                wait("vector", sPE, pe_t)
                wait("vector", sX, vt_pub[i - 2])
                opq("vector", lambda e, sl=sl: e.tensor_copy(
                    a_vt[:, sl * 132: sl * 132 + 64], PT[0:128, sl * 128: sl * 128 + 64]))
                opq("vector", lambda e, sl=sl: e.memset(a_vt[:, sl * 132 + 64: sl * 132 + 65], 1.0))
                opq("vector", lambda e, sl=sl: e.tensor_copy(
                    a_vt[:, sl * 132 + 66: sl * 132 + 66 + 64],
                    PT[0:128, sl * 128 + 64: sl * 128 + 128]))
                vt_dv[i] = opq("vector", lambda e, sl=sl: e.memset(
                    a_vt[:, sl * 132 + 66 + 64: sl * 132 + 66 + 65], 1.0))
                wait("gpsimd", sDV, vt_dv[i])
                wait("gpsimd", sX, vt_pub[i - 1])
                d_t = d_a1inA if g == 0 else d_a1inB
                for h in range(2):
                    kv = 2 * vh + h
                    for c in (2 * kv, 2 * kv + 1):
                        dma("gpsimd", d_t[c, :, 1024 + tt * 65: 1024 + tt * 65 + 65],
                            a_vt[:, sl * 132 + h * 66: sl * 132 + h * 66 + 65])
                vt_pub[i] = sX.v

            load_const_bf(a_kc1w, d_kc1, 2 * 3 * 64)
            kcw_dv = load_const_bf(a_kc2w, d_kc2, 2 * 2 * 3 * 128)
            wait("tensor", sDV, kcw_dv)
            k1_parts = []
            for j in range(2):
                pa = Pp[2 * (j % 2)]
                pb2 = Pp[2 * (j % 2) + 1]
                pe_k1 = 0
                for g2 in range(2):
                    rb = 64 * g2
                    for tap in range(3):
                        lhs = a_kc1w[rb:rb + 64, (j * 3 + tap) * 64: (j * 3 + tap) * 64 + 64]
                        rhs_c = (8 + j) * 516 + tap
                        MM(pa[rb:rb + 64, 0:512],
                           lhs, a_qkv[rb:rb + 64, rhs_c + 2: rhs_c + 2 + 512],
                           tap == 0, tap == 2, tp=(rb, rb))
                        pe_k1 = MM(pb2[rb:rb + 64, 0:2],
                                   lhs, a_qkv[rb:rb + 64, rhs_c: rhs_c + 2],
                                   tap == 0, tap == 2, tp=(rb, rb),
                                   inc=(g2 == 1 and tap == 2))
                wait("vector", sPE, pe_k1)
                opq("vector", lambda e, j=j, pa=pa: e.tensor_scalar(
                    a_k1[:, j * 514 + 2: j * 514 + 2 + 512], pa[0:128, 0:512],
                    sb_kc1bias[:, j:j + 1], None, OP.add))
                opq("vector", lambda e, j=j, pb2=pb2: e.tensor_scalar(
                    a_k1[:, j * 514: j * 514 + 2], pb2[0:128, 0:2],
                    sb_kc1bias[:, j:j + 1], None, OP.add))
            if g == 0:
                selfsync("vector")
                for j in range(2):
                    opq("vector", lambda e, j=j: e.tensor_scalar(
                        a_k1[:, j * 514: j * 514 + 2], a_k1[:, j * 514: j * 514 + 2],
                        sb_hscale[:, 0:1], None, OP.mult))
            k1_dv = now("vector")
            wait("tensor", sDV, k1_dv)

            kf_pub = 0
            prev_dv = 0
            prev_ac = 0
            for mq in range(2):
                pc2 = Pp[4 + (mq % 2)]
                wait("tensor", sDV, prev_dv)
                wait("tensor", sAC, prev_ac)
                first = True
                pe_c2 = 0
                for kq in range(2):
                    for tap in range(3):
                        lhs = a_kc2w[:, ((kq * 2 + mq) * 3 + tap) * 128:
                                     ((kq * 2 + mq) * 3 + tap) * 128 + 128]
                        pe_c2 = MM(pc2[0:128, 0:512], lhs,
                                   a_k1[:, kq * 514 + tap: kq * 514 + tap + 512],
                                   first, kq == 1 and tap == 2,
                                   inc=(kq == 1 and tap == 2))
                        first = False
                wait("vector", sPE, pe_c2)
                cvs = a_cv[:, (mq % 2) * 512:(mq % 2) * 512 + 512]
                opq("vector", lambda e, mq=mq, pc2=pc2, cvs=cvs: e.scalar_tensor_tensor(
                    cvs, pc2[0:128, 0:512], sb_kc2bias[:, mq:mq + 1],
                    a_means[:, 0:512], OP.add, OP.add))
                selfsync("vector")
                dv_c2 = opq("vector", lambda e, cvs=cvs: e.tensor_tensor(
                    a_sq[:, 0:512], cvs, cvs, OP.mult))
                wait("tensor", sDV, dv_c2)
                MM(Pp[6][0:2, 0:512], sb_ones2[:, :], a_sq[:, 0:512], True, True, inc=True)
                pe_sw = MM(Pp[2][0:128, 0:512], sb_perm[:, :], cvs, True, True, inc=True)
                wait("scalar", sPE, pe_sw)
                opq("scalar", lambda e: e.activation(a_rr[0:2, 0:512], Pp[6][0:2, 0:512], AF.Sqrt))
                selfsync("scalar")
                opq("scalar", lambda e: e.activation(a_rr[0:2, 0:512], a_rr[0:2, 0:512], AF.Ln))
                selfsync("scalar")
                ac_rr = opq("scalar", lambda e: e.activation(
                    a_rr[0:2, 0:512], a_rr[0:2, 0:512], AF.Exp, scale=-1.0))
                wait("vector", sAC, ac_rr)
                dv_kt = opq("vector", lambda e: e.tensor_scalar(
                    a_rr[0:2, 0:512], a_rr[0:2, 0:512], sb_ktcol[0:2, 0:1], None, OP.mult))
                wait("gpsimd", sDV, dv_kt)
                bcast(a_rr[0:1, 0:512], 512, [(a_rstdb[0:64, 0:512], 64)])
                bc2 = bcast(a_rr[1:2, 0:512], 512, [(a_rstdb[64:128, 0:512], 64)])
                wait("vector", sX, bc2)
                wait("vector", sPE, pe_sw)
                opq("vector", lambda e, cvs=cvs: e.tensor_tensor(
                    a_t1[:, 0:512], cvs, a_ropec[:, 0:512], OP.mult))
                opq("vector", lambda e: e.tensor_tensor(
                    a_t2[:, 0:512], Pp[2][0:128, 0:512], a_ropes[:, 0:512], OP.mult))
                selfsync("vector")
                opq("vector", lambda e: e.tensor_tensor(
                    a_t1[:, 0:512], a_t1[:, 0:512], a_t2[:, 0:512], OP.add))
                wait("vector", sX, kf_pub)
                selfsync("vector")
                dv_kf = opq("vector", lambda e: e.tensor_tensor(
                    a_qf[:, 0:512], a_t1[:, 0:512], a_rstdb[:, 0:512], OP.mult))
                wait("gpsimd", sDV, dv_kf)
                d_t = d_a1inA if g == 0 else d_a1inB
                for h in range(2):
                    kv = 2 * mq + h
                    for c in (2 * kv, 2 * kv + 1):
                        dma("gpsimd", d_t[c, 0:64, 512:1024],
                            a_qf[64 * h:64 * h + 64, 0:512])
                        dma("gpsimd", d_t[c, 64:128, 512:1024],
                            a_qf[64 * h:64 * h + 64, 0:512])
                wait("gpsimd", sX, sX.v)
                kf_pub = sX.v
                prev_dv = dv_kf
                prev_ac = ac_rr

            qcw_dv = load_const_bf(a_qc1w, d_qc1, 8 * 3 * 64)
            wait("tensor", sDV, qcw_dv)
            q1_ac = {-2: 0, -1: 0}
            for j in range(8):
                pa = Pp[2 * (j % 2)]
                pb2 = Pp[2 * (j % 2) + 1]
                wait("tensor", sDV, q1_ac[j - 2])
                pe_q1 = 0
                for g2 in range(2):
                    rb = 64 * g2
                    for tap in range(3):
                        lhs = a_qc1w[rb:rb + 64, (j * 3 + tap) * 64: (j * 3 + tap) * 64 + 64]
                        rhs_c = j * 516 + tap
                        MM(pa[rb:rb + 64, 0:512],
                           lhs, a_qkv[rb:rb + 64, rhs_c + 2: rhs_c + 2 + 512],
                           tap == 0, tap == 2, tp=(rb, rb))
                        pe_q1 = MM(pb2[rb:rb + 64, 0:2],
                                   lhs, a_qkv[rb:rb + 64, rhs_c: rhs_c + 2],
                                   tap == 0, tap == 2, tp=(rb, rb),
                                   inc=(g2 == 1 and tap == 2))
                wait("vector", sPE, pe_q1)
                opq("vector", lambda e, j=j, pa=pa: e.tensor_scalar(
                    a_q1[:, j * 514 + 2: j * 514 + 2 + 512], pa[0:128, 0:512],
                    sb_qc1bias[:, j:j + 1], None, OP.add))
                opq("vector", lambda e, j=j, pb2=pb2: e.tensor_scalar(
                    a_q1[:, j * 514: j * 514 + 2], pb2[0:128, 0:2],
                    sb_qc1bias[:, j:j + 1], None, OP.add))
                q1_ac[j] = now("vector")
            if g == 0:
                selfsync("vector")
                for j in range(8):
                    opq("vector", lambda e, j=j: e.tensor_scalar(
                        a_q1[:, j * 514: j * 514 + 2], a_q1[:, j * 514: j * 514 + 2],
                        sb_hscale[:, 0:1], None, OP.mult))
            q1_dv = now("vector")
            wait("tensor", sDV, q1_dv)

            qf_pub = kf_pub
            prev_dv = 0
            prev_ac = 0
            for mq in range(8):
                pc2 = Pp[4 + (mq % 2)]
                wait("tensor", sDV, prev_dv)
                wait("tensor", sAC, prev_ac)
                first = True
                pe_c2 = 0
                for pc in range(6):
                    piece, cv = ring_load_c("qc2", mq * 6 + pc,
                                            d_qc2[mq, :, pc * 512:(pc + 1) * 512],
                                            d_cb["qc2"][mq, :, pc * 512:(pc + 1) * 512], 512,
                                            eng="scalar")
                    wait("tensor", cv)
                    for ii in range(4):
                        idx = pc * 4 + ii
                        kq, tap = idx // 3, idx % 3
                        lhs = piece[:, ii * 128: ii * 128 + 128]
                        pe_c2 = MM(pc2[0:128, 0:512], lhs,
                                   a_q1[:, kq * 514 + tap: kq * 514 + tap + 512], first, idx == 23,
                                   inc=(ii == 3 or idx == 23))
                        first = False
                    ring_done()
                wait("vector", sPE, pe_c2)
                cvs = a_cv[:, (mq % 2) * 512:(mq % 2) * 512 + 512]
                opq("vector", lambda e, mq=mq, pc2=pc2, cvs=cvs: e.scalar_tensor_tensor(
                    cvs, pc2[0:128, 0:512], sb_qc2bias[:, mq:mq + 1],
                    a_means[:, 512:1024], OP.add, OP.add))
                selfsync("vector")
                dv_c2 = opq("vector", lambda e, cvs=cvs: e.tensor_tensor(
                    a_sq[:, 0:512], cvs, cvs, OP.mult))
                wait("tensor", sDV, dv_c2)
                MM(Pp[6][0:2, 0:512], sb_ones2[:, :], a_sq[:, 0:512], True, True, inc=True)
                pe_sw = MM(Pp[2][0:128, 0:512], sb_perm[:, :], cvs, True, True, inc=True)
                wait("scalar", sPE, pe_sw)
                opq("scalar", lambda e: e.activation(a_rr[0:2, 0:512], Pp[6][0:2, 0:512],
                                                     AF.Sqrt, scale=64.0))
                selfsync("scalar")
                opq("scalar", lambda e: e.activation(a_rr[0:2, 0:512], a_rr[0:2, 0:512], AF.Ln))
                selfsync("scalar")
                ac_rr = opq("scalar", lambda e: e.activation(
                    a_rr[0:2, 0:512], a_rr[0:2, 0:512], AF.Exp, scale=-1.0))
                wait("gpsimd", sAC, ac_rr)
                bcast(a_rr[0:1, 0:512], 512, [(a_rstdb[0:64, 0:512], 64)])
                bc2 = bcast(a_rr[1:2, 0:512], 512, [(a_rstdb[64:128, 0:512], 64)])
                wait("vector", sX, bc2)
                wait("vector", sPE, pe_sw)
                opq("vector", lambda e, cvs=cvs: e.tensor_tensor(
                    a_t1[:, 0:512], cvs, a_ropec[:, 0:512], OP.mult))
                opq("vector", lambda e: e.tensor_tensor(
                    a_t2[:, 0:512], Pp[2][0:128, 0:512], a_ropes[:, 0:512], OP.mult))
                selfsync("vector")
                opq("vector", lambda e: e.tensor_tensor(
                    a_t1[:, 0:512], a_t1[:, 0:512], a_t2[:, 0:512], OP.add))
                wait("vector", sX, qf_pub)
                selfsync("vector")
                dv_qf = opq("vector", lambda e: e.tensor_tensor(
                    a_qf[:, 0:512], a_t1[:, 0:512], a_rstdb[:, 0:512], OP.mult))
                wait("gpsimd", sDV, dv_qf)
                dma("gpsimd", (d_a1inA if g == 0 else d_a1inB)[mq, :, 0:512], a_qf[:, 0:512])
                wait("gpsimd", sX, sX.v)
                qf_pub = sX.v
                prev_dv = dv_qf
                prev_ac = ac_rr
            barrier()
            op("gpsimd", sCC, lambda e, g=g: e.collective_compute(
                "AllToAll", OP.bypass, replica_groups=[list(range(8))],
                ins=[(d_a1inA if g == 0 else d_a1inB)[:, :, :]],
                outs=[(d_a1outA if g == 0 else d_a1outB)[:, :, :]]))

    def attention():
        wait("sync", sW, sW.v)
        msk_ld = dma("sync", t_masks[:, :], d_masks[:, :])
        wait("vector", sW, msk_ld)
        wait("gpsimd", sCC, sCC.v)
        b_pe_done = 0
        ao_dv = {0: 0, 1: 0}
        ao_ac = {0: 0, 1: 0}
        ao_pub = [0]
        dn_x = [0]
        for b in range(2):
            wait("gpsimd", sPE, b_pe_done)
            asm_t = 0
            for qt in range(4):
                p = b * 4 + qt
                for gg, dsrc in ((0, d_a1outA), (1, d_a1outB)):
                    dma("gpsimd", t_Q[:, qt * 1024 + gg * 512: qt * 1024 + gg * 512 + 512],
                        dsrc[p, :, 0:512])
                    dma("gpsimd", t_K[:, qt * 1024 + gg * 512: qt * 1024 + gg * 512 + 512],
                        dsrc[p, :, 512:1024])
                    asm_t = dma(
                        "gpsimd",
                        t_VT.rearrange("p (t c) -> p t c", c=66)[:, qt * 8 + gg * 4: qt * 8 + gg * 4 + 4, 0:65],
                        dsrc[p, :, 1024:1284].rearrange("p (t c) -> p t c", c=65))
            wait("gpsimd", sX, asm_t)
            wait("tensor", sX, asm_t)
            exp_t = {}
            msk_t = {}
            av_t = {-2: 0, -1: 0}
            for qg in range(NQG):
                nkt = 4 * qg + 4
                AVa = Pp[4] if qg % 2 == 0 else Pp[6]
                AVb = Pp[5] if qg % 2 == 0 else PT32

                def scores(kt, qg=qg):
                    pA = Pp[2 * (kt % 2)]
                    pB = Pp[2 * (kt % 2) + 1]
                    MM(pA[0:128, 0:512], t_K[0:64, kt * 128: kt * 128 + 128],
                       t_Q[0:64, qg * 512: qg * 512 + 512], True, True, tp=(0, 0), inc=True)
                    sc = MM(pB[0:128, 0:512], t_K[64:128, kt * 128: kt * 128 + 128],
                            t_Q[64:128, qg * 512: qg * 512 + 512], True, True, tp=(64, 0), inc=True)
                    wait("scalar", sPE, sc)
                    wait("scalar", sPE, av_t[kt - 2])
                    opq("scalar", lambda e, kt=kt, pA=pA: e.activation(
                        t_at[:, (2 * kt % 4) * 512: (2 * kt % 4) * 512 + 512],
                        pA[0:128, 0:512], AF.Exp, bias=sb_expbias[:, 0:1]))
                    exp_t[kt] = opq("scalar", lambda e, kt=kt, pB=pB: e.activation(
                        t_at[:, ((2 * kt + 1) % 4) * 512: ((2 * kt + 1) % 4) * 512 + 512],
                        pB[0:128, 0:512], AF.Exp, bias=sb_expbias[:, 0:1]))
                    if kt >= 4 * qg:
                        j = kt - 4 * qg
                        wait("vector", sAC, exp_t[kt])
                        for s2 in (2 * kt % 4, (2 * kt + 1) % 4):
                            opq("vector", lambda e, s2=s2, j=j: e.tensor_tensor(
                                t_at[:, s2 * 512: s2 * 512 + 512],
                                t_at[:, s2 * 512: s2 * 512 + 512],
                                t_masks[:, j * 512: j * 512 + 512], OP.mult))
                        msk_t[kt] = now("vector")

                def av(kt, qg=qg, nkt=nkt, AVa=AVa, AVb=AVb):
                    if kt >= 4 * qg:
                        wait("tensor", sDV, msk_t[kt])
                    else:
                        wait("tensor", sAC, exp_t[kt])
                    if kt == 0:
                        wait("tensor", sDV, ao_dv[qg % 2])
                        wait("tensor", sAC, ao_ac[qg % 2])
                    MM(AVa[0:65, 0:512], t_VT[:, kt * 66: kt * 66 + 65],
                       t_at[:, (2 * kt % 4) * 512: (2 * kt % 4) * 512 + 512],
                       kt == 0, kt == nkt - 1, inc=True)
                    av_t[kt] = MM(AVb[0:65, 0:512], t_VT[:, kt * 66: kt * 66 + 65],
                                  t_at[:, ((2 * kt + 1) % 4) * 512: ((2 * kt + 1) % 4) * 512 + 512],
                                  kt == 0, kt == nkt - 1, inc=True)

                scores(0)
                for kt in range(1, nkt):
                    scores(kt)
                    av(kt - 1)
                av(nkt - 1)
                pe_fin = av_t[nkt - 1]
                wait("scalar", sPE, pe_fin)
                wait("scalar", sX, dn_x[0])
                opq("scalar", lambda e, AVa=AVa: e.activation(
                    t_dn[64:65, 0:512], AVa[64:65, 0:512], AF.Ln))
                opq("scalar", lambda e, AVb=AVb: e.activation(
                    t_dn[64:65, 512:1024], AVb[64:65, 0:512], AF.Ln))
                selfsync("scalar")
                opq("scalar", lambda e: e.activation(t_dn[64:65, 0:512], t_dn[64:65, 0:512],
                                                     AF.Exp, scale=-1.0))
                ac_dn = opq("scalar", lambda e: e.activation(
                    t_dn[64:65, 512:1024], t_dn[64:65, 512:1024], AF.Exp, scale=-1.0))
                ao_ac[qg % 2] = ac_dn
                wait("gpsimd", sAC, ac_dn)
                wait("gpsimd", sDV, ao_dv[(qg + 1) % 2])
                sl = bc_i[0] % 8
                bc_i[0] += 1
                t1b = dma("gpsimd", d_bc[sl:sl + 1, 0:1024].bitcast(BF16)[:, 0:1024] if False
                          else d_bc[sl:sl + 1, 0:1024], t_dn[64:65, 0:1024])
                wait("gpsimd", sX, t1b)
                dma("gpsimd", t_rdb[0:64, 0:512], d_bc[sl, 0:512].partition_broadcast(64))
                dma("gpsimd", t_rdb[0:64, 512:1024], d_bc[sl, 512:1024].partition_broadcast(64))
                wait("gpsimd", sX, sX.v)
                dn_x[0] = sX.v
                wait("vector", sX, dn_x[0])
                wait("vector", sPE, pe_fin)
                wait("vector", sX, ao_pub[0])
                opq("vector", lambda e, AVa=AVa: e.tensor_tensor(
                    t_ao[0:64, 0:512], AVa[0:64, 0:512], t_rdb[0:64, 0:512], OP.mult))
                opq("vector", lambda e, AVb=AVb: e.tensor_tensor(
                    t_ao[0:64, 512:1024], AVb[0:64, 0:512], t_rdb[0:64, 512:1024], OP.mult))
                ao_dv[qg % 2] = now("vector")
                wait("gpsimd", sDV, ao_dv[qg % 2])
                pdst = b * 4 + qg // 2
                cb = (qg % 2) * 512
                dma("gpsimd", d_a2in[pdst, 0:64, cb:cb + 512], t_ao[0:64, 0:512])
                ao_pub[0] = dma("gpsimd", d_a2in[pdst, 64:128, cb:cb + 512], t_ao[0:64, 512:1024])
                wait("gpsimd", sX, ao_pub[0])
            b_pe_done = now("tensor")
        wait("gpsimd", sX, sX.v)
        cc2 = op("gpsimd", sCC, lambda e: e.collective_compute(
            "AllToAll", OP.bypass, replica_groups=[list(range(8))],
            ins=[d_a2in[:, :, :]], outs=[d_a2out[:, :, :]]))
        wait("gpsimd", sCC, cc2)
        barrier()

    def wo():
        for blk in range(2):
            asm_t = 0
            for c in range(8):
                asm_t = dma("gpsimd", a_af[:, c * 512:(c + 1) * 512],
                            d_a2out[c, :, blk * 512: blk * 512 + 512])
            wait("gpsimd", sX, asm_t)
            wait("tensor", sX, asm_t)
            ev_dv = {-2: 0, -1: 0}
            for mq in range(KC):
                ps = Pp[2 * (mq % 2)]
                wait("tensor", sDV, ev_dv[mq - 2])
                first = True
                pe_mq = 0
                for pc in range(2):
                    piece, cv = ring_load_c("wo", mq * 2 + pc,
                                            d_wo[mq, :, pc * 512:(pc + 1) * 512],
                                            d_cb["wo"][mq, :, pc * 512:(pc + 1) * 512], 512)
                    wait("tensor", cv)
                    for kk in range(4):
                        kq = pc * 4 + kk
                        pe_mq = MM(ps[0:128, 0:512], piece[:, kk * 128: kk * 128 + 128],
                                   a_af[:, kq * 512:(kq + 1) * 512], first, kq == 7,
                                   inc=(kk == 3))
                        first = False
                    ring_done()
                wait("vector", sPE, pe_mq)
                col = mq * W + HALO + blk * 512
                opq("vector", lambda e, col=col, ps=ps: e.tensor_tensor(
                    sb_xT[:, col: col + 512], sb_xT[:, col: col + 512], ps[0:128, 0:512], OP.add))
                ev_dv[mq] = now("vector")
            barrier()

    if _PHASES >= 1:
        ffn(0)
    if _PHASES >= 2:
        qkv_conv()
        attention()
        wo()
    if _PHASES >= 3:
        ffn(1)

    dv_all = now("vector")
    wait("gpsimd", sDV, dv_all)
    for k in range(KC):
        dma("gpsimd", d_yT[k * 128:(k + 1) * 128, :], sb_xT[:, k * W + HALO:(k + 1) * W], ctr=sST)
    wait("gpsimd", sST, sST.v)

    with nc.Block() as block:
        @block.sync
        def _(e):
            for f in EV["sync"]:
                f(e)

        @block.tensor
        def _(e):
            for f in EV["tensor"]:
                f(e)

        @block.vector
        def _(e):
            for f in EV["vector"]:
                f(e)

        @block.scalar
        def _(e):
            for f in EV["scalar"]:
                f(e)

        @block.gpsimd
        def _(e):
            for f in EV["gpsimd"]:
                f(e)

    return nc


_prog = None


def _get_prog():
    global _prog
    if _prog is None:
        _prog = build()
    return _prog


_consts = None


def _get_consts():
    global _consts
    if _consts is not None:
        return _consts
    ident = np.eye(128, dtype=BF)
    perm = np.zeros((128, 128), dtype=BF)
    for p2 in range(128):
        perm[p2, p2 ^ 1] = 1.0
    sumhead = np.zeros((128, 64), dtype=BF)
    for d in range(64):
        sumhead[d, d] = 1.0
        sumhead[d + 64, d] = 1.0
    ones1 = np.ones((128, 1), dtype=BF)
    ones2 = np.zeros((128, 2), dtype=BF)
    ones2[0:64, 0] = 1.0
    ones2[64:128, 1] = 1.0
    pairsel = np.zeros((16, 8 * 128), dtype=BF)
    for p2 in range(8):
        for d in range(128):
            pairsel[2 * p2 + d // 64, p2 * 128 + d] = 1.0
    masks = np.zeros((128, 4 * 512), dtype=BF)
    for j in range(4):
        for kk in range(128):
            masks[kk, j * 512 + 128 * j + kk: (j + 1) * 512] = 1.0
    oner64 = np.zeros((128, 64), dtype=BF)
    oner64[64, :] = 1.0
    _consts = dict(ident_bf=ident, perm_bf=perm, sumhead_bf=sumhead, ones1_bf=ones1,
                   ones2_bf=ones2, pairsel_bf=pairsel, masks_bf=masks, oner64_bf=oner64)
    return _consts


def _block_weights(inp):
    wq = np.ascontiguousarray(
        inp["L0_attn_w_qkv"].reshape(12, 128, 32, 128).transpose(0, 3, 2, 1)
    ).reshape(12, 128, KC * 128)
    bw = np.ascontiguousarray(
        inp["L0_ffn_base_w"].reshape(32, 128, 32, 128).transpose(0, 3, 2, 1)
    ).reshape(KC, 128, KC * 128)
    wo = np.ascontiguousarray(
        inp["L0_attn_w_o"].reshape(32, 128, 8, 128).transpose(0, 3, 2, 1)
    ).reshape(KC, 128, 8 * 128)
    be = np.ascontiguousarray(
        inp["L0_ffn_B"].transpose(0, 2, 1).reshape(8, 128, 4096).reshape(8, 128, 32, 128)
        .transpose(2, 1, 0, 3)
    ).reshape(KC, 128, 8 * 128)
    qc2 = np.ascontiguousarray(
        inp["L0_attn_qc2_w"].reshape(8, 128, 8, 128, 3).transpose(0, 3, 2, 4, 1)
    ).reshape(8, 128, 8 * 3 * 128)
    ar = np.zeros((128, KC * 128), np.float32)
    AR = np.concatenate([inp["L0_ffn_A"], inp["L0_ffn_router_w"]], axis=0)
    art = AR.T.reshape(32, 128, 80)
    ar3 = ar.reshape(128, 32, 128)
    ar3[:, :, 0:80] = art.transpose(1, 0, 2)
    qc1 = np.ascontiguousarray(
        inp["L0_attn_qc1_w"].reshape(8, 2, 64, 64, 3).transpose(1, 3, 0, 4, 2)
    ).reshape(128, 8 * 3 * 64)
    kc1 = np.ascontiguousarray(
        inp["L0_attn_kc1_w"].reshape(2, 2, 64, 64, 3).transpose(1, 3, 0, 4, 2)
    ).reshape(128, 2 * 3 * 64)
    kc2 = np.ascontiguousarray(
        inp["L0_attn_kc2_w"].reshape(2, 128, 2, 128, 3).transpose(3, 2, 0, 4, 1)
    ).reshape(128, 2 * 2 * 3 * 128)
    return dict(
        wqkv_b=wq, base_b=bw, wo_b=wo, bexp_b=be, qc2_b=qc2, ar_b=ar,
        qc1_b=qc1, kc1_b=kc1, kc2_b=kc2,
        qc1_bias=np.ascontiguousarray(inp["L0_attn_qc1_b"].reshape(8, 128).T),
        kc1_bias=np.ascontiguousarray(inp["L0_attn_kc1_b"].reshape(2, 128).T),
        qc2_bias=np.ascontiguousarray(inp["L0_attn_qc2_b"].reshape(8, 128).T),
        kc2_bias=np.ascontiguousarray(inp["L0_attn_kc2_b"].reshape(2, 128).T),
        base_bias=np.ascontiguousarray(inp["L0_ffn_base_b"].reshape(32, 128).T),
        key_temp=np.asarray(inp["L0_attn_key_temp"], np.float32).reshape(1),
    )


def _rope_tables(qstart):
    inv = (1.0 / 10000.0 ** (np.arange(0, HD, 2, dtype=np.float64) / HD))
    t = np.arange(qstart, qstart + TOKQ, dtype=np.float64)
    ang = t[None, :] * inv[:, None]
    cos = np.cos(ang)
    sin = np.sin(ang)
    cos128 = np.zeros((128, TOKQ), np.float32)
    sin128 = np.zeros((128, TOKQ), np.float32)
    for r in range(128):
        i = (r % 64) // 2
        cos128[r] = cos[i]
        sin128[r] = sin[i] if (r % 2 == 1) else -sin[i]
    return cos128, sin128


def kernel(**inputs):
    inp = {k: np.asarray(v) for k, v in inputs.items()}
    x = inp["L0_attn_x"].astype(np.float32)
    wmaps = _block_weights(inp)
    consts = _get_consts()
    in_maps = []
    for c in range(8):
        b, qt = c // 4, c % 4
        qstart = qt * TOKQ
        xs = np.zeros((D, W), np.float32)
        lo = max(0, qstart - HALO)
        nh = qstart - lo
        if nh > 0:
            xs[:, HALO - nh:HALO] = x[b, lo:qstart, :].T
        xs[:, HALO:] = x[b, qstart:qstart + TOKQ, :].T
        cos128, sin128 = _rope_tables(qstart)
        m = dict(wmaps)
        m.update(consts)
        m["xT"] = xs
        m["rope_cos"] = cos128
        m["rope_sin"] = sin128
        m["halo_scale"] = np.array([0.0 if qt == 0 else 1.0], np.float32)
        in_maps.append(m)
    nc = _get_prog()
    res = run_bass_kernel_spmd(nc, in_maps, core_ids=list(range(8)))
    out = np.empty((2, S, D), np.float32)
    for c in range(8):
        b, qt = c // 4, c % 4
        out[b, qt * TOKQ:(qt + 1) * TOKQ, :] = res.results[c]["yT"].T
    return out
